# version 16
# baseline (speedup 1.0000x reference)
import numpy as np
import ml_dtypes
import concourse.bass as bass
import concourse.mybir as mybir
from concourse.bass_utils import run_bass_kernel_spmd

F32 = mybir.dt.float32
BF16 = mybir.dt.bfloat16
AF = mybir.ActivationFunctionType
ALU = mybir.AluOpType
AX = mybir.AxisListType

D = 1024
S = 2048
H = 16
HD = 64
E = 8
HF = 2048
NKC = 8
NTT = 16
NQC = 4
NHF = 16
EPS = 1e-5
DEPTH = 1

ENGS = ("te", "ve", "se", "dg", "dq")
STEP = {"te": 1, "ve": 1, "se": 1, "dg": 16, "dq": 16}


class Buf:
    __slots__ = ("w", "r", "name")

    def __init__(self, name=""):
        self.w = {}
        self.r = {}
        self.name = name


class Planner:
    def __init__(self):
        self.ops = {e: [] for e in ENGS}
        self.ghost = {}
        self.scope_bufs = None

    QUEUES = ("dg", "dq")

    def buf(self, name=""):
        b = Buf(name)
        if self.ghost:
            b.r = dict(self.ghost)
        if self.scope_bufs is not None:
            self.scope_bufs.append(b)
        return b

    def begin_scope(self):
        assert self.scope_bufs is None
        self.scope_bufs = []

    def end_scope(self):
        mx = dict(self.ghost)
        for b in self.scope_bufs:
            for e, i in list(b.r.items()) + list(b.w.items()):
                if i > mx.get(e, -1):
                    mx[e] = i
        self.ghost = mx
        self.scope_bufs = None

    def op(self, eng, emit, reads=(), writes=(), tag="", sync=False):
        idx = len(self.ops[eng])
        deps = {}
        inq = eng in self.QUEUES

        def add(e, i, force=False):
            if (e != eng or force or e in self.QUEUES) and i > deps.get(e, -1):
                deps[e] = i

        for b in reads:
            for e, i in b.w.items():
                add(e, i, force=True)
        for b in writes:
            for e, i in b.w.items():
                add(e, i, force=sync)
            for e, i in b.r.items():
                add(e, i, force=True)
        self.ops[eng].append((emit, deps, tag))
        for b in reads:
            if b.r.get(eng, -1) < idx:
                b.r[eng] = idx
        for b in writes:
            b.w[eng] = idx
            b.r = {}
            if inq or len(b.w) > 1:
                pass
            else:
                b.w = {eng: idx}
        return (eng, idx)

    def check_deadlock(self):
        ptr = {e: 0 for e in ENGS}
        total = sum(len(self.ops[e]) for e in ENGS)
        done = 0
        progress = True
        while progress:
            progress = False
            for e in ENGS:
                while ptr[e] < len(self.ops[e]):
                    _, deps, tag = self.ops[e][ptr[e]]
                    if all(ptr[de] > di for de, di in deps.items()):
                        ptr[e] += 1
                        done += 1
                        progress = True
                    else:
                        break
        if done != total:
            stuck = {e: (ptr[e], len(self.ops[e]),
                         self.ops[e][ptr[e]][2] if ptr[e] < len(self.ops[e]) else "")
                     for e in ENGS}
            raise RuntimeError(f"planner deadlock: {stuck}")

    NSEM = 4

    def finalize(self):
        self.check_deadlock()
        incset = {e: set() for e in ENGS}
        for e in ENGS:
            for (_, deps, _) in self.ops[e]:
                for de, di in deps.items():
                    incset[de].add(di)
        self.rank = {}
        self.incset = {}
        for e in ("te", "ve", "se"):
            srt = sorted(incset[e])
            self.incset[e] = set(srt)
            r = {}
            for pos, i in enumerate(srt):
                r[i] = pos + 1
            self.rank[e] = r
        self.qsem = {}
        for e in ("dg", "dq"):
            n = len(self.ops[e])
            self.qsem[e] = [(i % self.NSEM, i // self.NSEM + 1) for i in range(n)]

    def dep_sem(self, e, i):
        if e in ("dg", "dq"):
            slot, k = self.qsem[e][i]
            return (f"{e}{slot}", 16 * k)
        return (e, self.rank[e][i])

    def emit(self, eng, obj, sems):
        waited = {}

        def wait(key, v):
            if v > waited.get(key, 0):
                obj.wait_ge(sems[key], v)
                waited[key] = v

        if eng in ("dg", "dq"):
            for idx, (emit_fn, deps, tag) in enumerate(self.ops[eng]):
                slot, k = self.qsem[eng][idx]
                if k > 1:
                    wait(f"{eng}{slot}", 16 * (k - 1))
                for de, di in deps.items():
                    wait(*self.dep_sem(de, di))
                inst = emit_fn(obj)
                inst.then_inc(sems[f"{eng}{slot}"], 16)
        else:
            inc = self.incset[eng]
            for idx, (emit_fn, deps, tag) in enumerate(self.ops[eng]):
                for de, di in deps.items():
                    wait(*self.dep_sem(de, di))
                inst = emit_fn(obj)
                if idx in inc:
                    inst.then_inc(sems[eng], 1)


class Arena:

    def __init__(self, nc, nbytes):
        r = nc.bump_sbuf(nbytes)
        assert r is not None, "arena alloc failed"
        self.nc = nc
        self.base = r[0]
        self.size = nbytes
        self.cur = 0
        self.n = 0

    def reset(self):
        self.cur = 0

    def alloc(self, shape, dtype):
        nb = int(np.prod(shape[1:])) * mybir.dt.size(dtype)
        nb = (nb + 31) & ~31
        assert self.cur + nb <= self.size, f"arena overflow {self.cur + nb} > {self.size}"
        self.n += 1
        self.last_off = self.base + self.cur
        t = self.nc.alloc_sbuf_tensor_at(f"ar{self.n}", list(shape), dtype,
                                         offset=self.base + self.cur)
        self.cur += nb
        return t


def _host_consts():
    pos = np.arange(S, dtype=np.float64)
    inv = np.power(10000.0, -2.0 * np.arange(HD // 2, dtype=np.float64) / HD)
    ang = pos[:, None] * inv[None, :]
    cos2 = np.repeat(np.cos(ang), 2, axis=-1)
    sin2 = np.repeat(np.sin(ang), 2, axis=-1)
    cosT = np.empty((128, S), dtype=np.float32)
    sinT = np.empty((128, S), dtype=np.float32)
    cosT[0:64, :] = cos2.T
    cosT[64:128, :] = cos2.T
    sinT[0:64, :] = sin2.T
    sinT[64:128, :] = sin2.T
    prot = np.zeros((128, 128), dtype=np.float32)
    for blk in (0, 64):
        for i in range(32):
            prot[blk + 2 * i + 1, blk + 2 * i] = -1.0
            prot[blk + 2 * i, blk + 2 * i + 1] = 1.0
    return (cosT.astype(ml_dtypes.bfloat16), sinT.astype(ml_dtypes.bfloat16),
            prot.astype(ml_dtypes.bfloat16))


def build_program(taps=(), stop_after=None):
    nc = bass.Bass("TRN2", target_bir_lowering=False)
    P = Planner()

    dt = {}
    def din(name, shape):
        dt[name] = nc.dram_tensor(name, list(shape), F32, kind="ExternalInput")
        return dt[name]

    x_in = din("x", (S, D))
    c_in = din("c", (D,))
    wmod = din("wmod", (D, 6 * D))
    wq = din("wq", (D, D)); bq = din("bq", (D,))
    wk = din("wk", (D, D)); bk = din("bk", (D,))
    wv = din("wv", (D, D)); bv = din("bv", (D,))
    router_w = din("router_w", (D, E)); router_b = din("router_b", (E,))
    sw1 = din("sw1", (D, HF)); sb1 = din("sb1", (HF,))
    sw2 = din("sw2", (HF, D)); sb2 = din("sb2", (D,))
    rw1 = din("rw1", (E, D, HF)); rb1 = din("rb1", (E, HF))
    rw2 = din("rw2", (E, HF, D)); rb2 = din("rb2", (E, D))
    x_out = nc.dram_tensor("out", [S, D], F32, kind="ExternalOutput")

    cosT_np, sinT_np, prot_np = _host_consts()
    d_cos = nc.inline_tensor(cosT_np, "c_cos")
    d_sin = nc.inline_tensor(sinT_np, "c_sin")
    d_prot = nc.inline_tensor(prot_np, "c_prot")
    d_idf = nc.inline_tensor(np.eye(128, dtype=np.float32), "c_idf")
    d_idb = nc.inline_tensor(np.eye(128, dtype=ml_dtypes.bfloat16), "c_idb")
    d_or = nc.inline_tensor(np.ones((1, 128), dtype=ml_dtypes.bfloat16), "c_or")
    d_oc = nc.inline_tensor(np.ones((128, 1), dtype=ml_dtypes.bfloat16), "c_oc")
    d_orow2k = nc.inline_tensor(np.ones((1, S), dtype=ml_dtypes.bfloat16), "c_or2k")

    A = nc.alloc_sbuf_tensor
    xT = [A(f"xT{k}", [128, S], F32) for k in range(NKC)]
    hT = [A(f"hT{k}", [128, S], BF16) for k in range(NKC)]
    identF = A("identF", [128, 128], F32)
    identB = A("identB", [128, 128], BF16)
    prot = A("prot", [128, 128], BF16)
    ones_row = A("ones_row", [1, 128], BF16)
    ones_col = A("ones_col", [128, 1], BF16)
    modfm = A("modfm", [128, 6 * 8], F32)
    bqkv = A("bqkv", [128, 24], F32)
    sb1fm = A("sb1fm", [128, 16], F32)
    rb1fm = A("rb1fm", [128, 32], F32)
    rbrow = A("rbrow", [8, 1], F32)
    rwrout = A("rwrout", [128, 64], BF16)
    stage16 = A("stage16", [16, 128], F32)
    fmtmp = A("fmtmp", [128, 16], F32)
    smscr = A("smscr", [128, 8], F32)
    sme = A("sme", [128, 8], F32)
    smg = A("smg", [128, 8], F32)
    qnt = A("qnt", [1, 16], F32)
    qnb = A("qnb", [1, 2], BF16)
    expb = A("expb", [128, 2], F32)
    cvec = A("cvec", [128, 2], F32)
    ones_wide = A("ones_wide", [128, 64], BF16)

    pw = [nc.alloc_psum_tensor(f"pw{i}", [128, 1024], F32) for i in range(2)]
    pb = []
    for i in range(2):
        pb += [pw[i][:, 0:512], pw[i][:, 512:1024]]
    pb += [nc.alloc_psum_tensor(f"pb{i}", [128, 512], F32)[:, :] for i in range(4, 8)]
    pwf = [pw[0][:, :], pw[1][:, :]]

    AR = Arena(nc, 107 * 1024)

    sems = {e: nc.alloc_semaphore(f"sem_{e}") for e in ("te", "ve", "se")}
    for e in ("dg", "dq"):
        for s in range(Planner.NSEM):
            sems[f"{e}{s}"] = nc.alloc_semaphore(f"sem_{e}{s}")

    B = P.buf
    b_xT = [B(f"xT{k}") for k in range(NKC)]
    b_hT = [B(f"hT{k}") for k in range(NKC)]
    b_consts = B("consts")
    b_modfm = B("modfm")
    b_bqkv = B("bqkv")
    b_sb1fm = B("sb1fm")
    b_rb1fm = [B("rb1fm0"), B("rb1fm1")]
    b_rbrow = B("rbrow")
    b_rwrout = B("rwrout")
    b_stage16 = B("stage16")
    b_fmtmp = B("fmtmp")
    b_smscr = B("smscr"); b_sme = B("sme"); b_smg = B("smg")
    b_qnt = B("qnt"); b_qnb = B("qnb"); b_expb = B("expb")
    b_cvec = B("cvec")
    b_pb = [B(f"pb{i}") for i in range(8)]
    b_out = B("outdma")

    tap_reg = {}

    def reg(name, tensor, buf):
        tap_reg[name] = (tensor, buf)

    for dst, src in ((identF, d_idf), (identB, d_idb), (prot, d_prot),
                     (ones_row, d_or), (ones_col, d_oc)):
        P.op("dq", lambda o, d=dst, s=src: o.dma_start(d[:, :], s[:, :]),
             writes=[b_consts], tag="const")
    P.op("dq", lambda o: o.dma_start(rbrow[:, :], router_b.rearrange("(p one) -> p one", one=1)),
         writes=[b_rbrow])
    P.op("ve", lambda o: o.memset(cvec[:, 0:1], EPS), writes=[b_cvec])
    P.op("ve", lambda o: o.memset(cvec[:, 1:2], 1e-30), writes=[b_cvec])
    P.op("ve", lambda o: o.memset(ones_wide[:, :], 1.0), writes=[b_consts])
    P.op("dg", lambda o: o.dma_start(
        rwrout[:, :].rearrange("p (kc e) -> p kc e", kc=8),
        router_w.rearrange("(kc p) e -> p kc e", p=128)), writes=[b_rwrout])

    def load_fm(vec_dram, n, dst_ap_fn, bufw, tag):
        P.op("dq", lambda o: o.dma_start(stage16[0:n, :],
             vec_dram.rearrange("(j p) -> j p", p=128)),
             reads=[], writes=[b_stage16], tag=tag)
        P.op("te", lambda o: o.transpose(pb[7][:, 0:n], stage16[0:n, :], identF[0:n, 0:n]),
             reads=[b_stage16, b_consts], writes=[b_pb[7]], tag=tag)
        P.op("ve", lambda o: o.tensor_copy(dst_ap_fn(), pb[7][:, 0:n]),
             reads=[b_pb[7]], writes=[bufw], tag=tag)

    load_fm(bq, 8, lambda: bqkv[:, 0:8], b_bqkv, "bq")
    load_fm(bk, 8, lambda: bqkv[:, 8:16], b_bqkv, "bk")
    load_fm(bv, 8, lambda: bqkv[:, 16:24], b_bqkv, "bv")
    load_fm(sb1, 16, lambda: sb1fm[:, :], b_sb1fm, "sb1")

    P.begin_scope()
    AR.reset()
    xin = [AR.alloc([128, D], F32) for _ in range(3)]
    b_xin = [B(f"xin{i}") for i in range(3)]
    silu_fm = AR.alloc([128, 8], F32); b_silufm = B("silufm")
    silu_rep = AR.alloc([128, 8 * 128], F32); b_silurep = B("silurep")
    wstage = [AR.alloc([128, 8 * 512], F32) for _ in range(2)]
    b_wstage = [B("ws0"), B("ws1")]
    modsb = [AR.alloc([128, 512], F32) for _ in range(2)]
    b_modsb = [B("modsb0"), B("modsb1")]

    P.op("dq", lambda o: o.dma_start(stage16[0:8, :], c_in.rearrange("(j p) -> j p", p=128)),
         writes=[b_stage16])
    P.op("te", lambda o: o.transpose(pb[7][:, 0:8], stage16[0:8, :], identF[0:8, 0:8]),
         reads=[b_stage16, b_consts], writes=[b_pb[7]])
    P.op("se", lambda o: o.activation(silu_fm[:, :], pb[7][:, 0:8], AF.Silu),
         reads=[b_pb[7]], writes=[b_silufm])
    for kc in range(NKC):
        P.op("ve", lambda o, k=kc: o.tensor_scalar(
            silu_rep[:, k * 128:(k + 1) * 128], identF[:, :],
            0.0, silu_fm[:, k:k + 1], op0=ALU.mult, op1=ALU.add),
            reads=[b_silufm, b_consts], writes=[b_silurep])

    def emit_x_tile(t):
        bi = b_xin[t % 3]
        P.op("dq", lambda o, t=t: o.dma_start(xin[t % 3][:, :], x_in[t * 128:(t + 1) * 128, :]),
             writes=[bi], tag=f"xin{t}")
        for kc in range(NKC):
            pbank = b_pb[6]
            P.op("te", lambda o, t=t, k=kc: o.transpose(
                pb[6][:, 0:128], xin[t % 3][:, k * 128:(k + 1) * 128], identF[:, :]),
                reads=[bi, b_consts], writes=[pbank], tag=f"xT{t}.{kc}")
            P.op("se", lambda o, t=t, k=kc: o.copy(
                xT[k][:, t * 128:(t + 1) * 128], pb[6][:, 0:128]),
                reads=[pbank], writes=[b_xT[kc]], tag=f"xTe{t}.{kc}")

    def emit_mod_group(ncg):
        ws = b_wstage[ncg % 2]
        P.op("dg", lambda o, g=ncg: o.dma_start(
            wstage[g % 2][:, :].rearrange("p (kc n) -> p kc n", kc=8),
            wmod[:, g * 512:(g + 1) * 512].rearrange("(kc p) n -> p kc n", p=128)),
            writes=[ws], tag=f"wmod{ncg}")
        for kc in range(NKC):
            P.op("te", lambda o, g=ncg, k=kc: o.matmul(
                pb[5][:, :], silu_rep[:, k * 128:(k + 1) * 128],
                wstage[g % 2][:, k * 512:(k + 1) * 512],
                start=(k == 0), stop=(k == NKC - 1)),
                reads=[b_silurep, ws], writes=[b_pb[5]], tag=f"mod{ncg}")
        P.op("ve", lambda o, g=ncg: o.tensor_copy(modsb[g % 2][:, :], pb[5][:, :]),
             reads=[b_pb[5]], writes=[b_modsb[ncg % 2]], tag=f"modsb{ncg}")
        for j in range(4):
            f = ncg * 4 + j
            v = f // 8
            cc = f % 8
            P.op("te", lambda o, g=ncg, j=j: o.transpose(
                pb[4][:, 0:128], modsb[g % 2][:, j * 128:(j + 1) * 128], identF[:, :]),
                reads=[b_modsb[ncg % 2], b_consts], writes=[b_pb[4]], tag=f"modT{f}")
            addc = 1.0 if v in (1, 4) else 0.0
            P.op("ve", lambda o, v=v, cc=cc, a=addc: o.tensor_scalar(
                modfm[:, v * 8 + cc:v * 8 + cc + 1], pb[4][:, 0:1], a, None, op0=ALU.add),
                reads=[b_pb[4]], writes=[b_modfm], tag=f"modfm{f}")

    for i in range(12):
        emit_mod_group(i)
        emit_x_tile(i)
    for t in range(12, 16):
        emit_x_tile(t)

    for v in (2, 5):
        sl = modfm[:, v * 8:(v + 1) * 8]
        P.op("se", lambda o, sl=sl: o.activation(fmtmp[:, 0:8], sl, AF.Abs),
             reads=[b_modfm], writes=[b_fmtmp])
        P.op("ve", lambda o: o.tensor_scalar(fmtmp[:, 0:8], fmtmp[:, 0:8], 0.25, None, op0=ALU.max),
             writes=[b_fmtmp])
        P.op("ve", lambda o, sl=sl: o.tensor_scalar(
            fmtmp[:, 8:16], sl, 0.0, 2.0, op0=ALU.is_ge, op1=ALU.mult),
            reads=[b_modfm], writes=[b_fmtmp], sync=True)
        P.op("ve", lambda o, sl=sl: o.tensor_tensor(sl, fmtmp[:, 8:16], fmtmp[:, 0:8], op=ALU.mult),
             reads=[b_fmtmp], writes=[b_modfm], sync=True)
        P.op("ve", lambda o, sl=sl: o.tensor_tensor(sl, sl, fmtmp[:, 0:8], op=ALU.subtract),
             reads=[b_fmtmp], writes=[b_modfm], sync=True)
    P.end_scope()

    def layer_norm(which):
        P.begin_scope()
        AR.reset()
        xbf = [AR.alloc([128, 512], BF16) for _ in range(2)]
        sqbf = [AR.alloc([128, 512], BF16) for _ in range(2)]
        b_xbf = [B("xbf0"), B("xbf1")]
        b_sqbf = [B("sqbf0"), B("sqbf1")]
        m2row = AR.alloc([1, 512], F32); b_m2 = B("m2")
        srow = AR.alloc([1, 512], F32); b_sr = B("sr")
        verow = AR.alloc([1, 512], F32); b_ve_ = B("verow")
        r0row = AR.alloc([1, 512], F32); b_r0 = B("r0")
        t1row = AR.alloc([1, 512], F32); b_t1 = B("t1")
        rrow = AR.alloc([1, 512], F32); b_rr = B("rr")
        mrrow = AR.alloc([1, 512], F32); b_mr = B("mr")
        rbfr = AR.alloc([1, 512], BF16); b_rbf = B("rbf")
        mrbfr = AR.alloc([1, 512], BF16); b_mrbf = B("mrbf")
        rrep = AR.alloc([128, S], F32); b_rrep = B("rrep")
        mrrep = AR.alloc([128, S], F32); b_mrrep = B("mrrep")
        ttile = AR.alloc([128, S], F32); b_tt = B("ttile")

        pS, pQ, pR = b_pb[0], b_pb[1], b_pb[2]
        if which == 1:
            ca = lambda k: modfm[:, 1 * 8 + k:1 * 8 + k + 1]
            sa = lambda k: modfm[:, 0 * 8 + k:0 * 8 + k + 1]
        else:
            ca = lambda k: modfm[:, 4 * 8 + k:4 * 8 + k + 1]
            sa = lambda k: modfm[:, 3 * 8 + k:3 * 8 + k + 1]

        for ch in range(NQC):
            cs = slice(ch * 512, (ch + 1) * 512)
            for kc in range(NKC):
                P.op("se", lambda o, k=kc, c=cs: o.copy(xbf[k % 2][:, :], xT[k][:, c]),
                     reads=[b_xT[kc]], writes=[b_xbf[kc % 2]])
                P.op("se", lambda o, k=kc, c=cs: o.activation(sqbf[k % 2][:, :], xT[k][:, c], AF.Square),
                     reads=[b_xT[kc]], writes=[b_sqbf[kc % 2]])
                P.op("te", lambda o, k=kc: o.matmul(pb[0][0:1, :], ones_col[:, :], xbf[k % 2][:, :],
                     start=(k == 0), stop=(k == NKC - 1)),
                     reads=[b_xbf[kc % 2], b_consts], writes=[pS])
                P.op("te", lambda o, k=kc: o.matmul(pb[1][0:1, :], ones_col[:, :], sqbf[k % 2][:, :],
                     start=(k == 0), stop=(k == NKC - 1)),
                     reads=[b_sqbf[kc % 2], b_consts], writes=[pQ])
            P.op("ve", lambda o: o.tensor_scalar(pb[0][0:1, :], pb[0][0:1, :], 1.0 / D, None, op0=ALU.mult),
                 reads=[pS], writes=[pS])
            P.op("ve", lambda o: o.tensor_scalar(pb[1][0:1, :], pb[1][0:1, :], 1.0 / D, None, op0=ALU.mult),
                 reads=[pQ], writes=[pQ])
            P.op("se", lambda o: o.activation(m2row[0:1, :], pb[0][0:1, :], AF.Square),
                 reads=[pS], writes=[b_m2])
            P.op("ve", lambda o: o.tensor_tensor(pb[1][0:1, :], pb[1][0:1, :], m2row[0:1, :], op=ALU.subtract),
                 reads=[pQ, b_m2], writes=[pQ])
            P.op("se", lambda o: o.activation(srow[0:1, :], pb[1][0:1, :], AF.Ln, bias=cvec[0:1, 0:1]),
                 reads=[pQ, b_cvec], writes=[b_sr])
            P.op("se", lambda o: o.activation(rrow[0:1, :], srow[0:1, :], AF.Exp, scale=-0.5),
                 reads=[b_sr], writes=[b_rr])
            P.op("se", lambda o: o.activation(rbfr[0:1, :], srow[0:1, :], AF.Exp, scale=-0.5),
                 reads=[b_sr], writes=[b_rbf])
            P.op("ve", lambda o: o.tensor_tensor(mrrow[0:1, :], pb[0][0:1, :], rrow[0:1, :], op=ALU.mult),
                 reads=[pS, b_rr], writes=[b_mr])
            P.op("ve", lambda o: o.tensor_copy(mrbfr[0:1, :], mrrow[0:1, :]),
                 reads=[b_mr], writes=[b_mrbf])
            P.op("te", lambda o: o.matmul(pb[2][:, :], ones_row[:, :], rbfr[0:1, :], start=True, stop=True),
                 reads=[b_rbf, b_consts], writes=[pR])
            P.op("ve", lambda o, c=cs: o.tensor_copy(rrep[:, c], pb[2][:, :]),
                 reads=[pR], writes=[b_rrep])
            P.op("te", lambda o: o.matmul(pb[2][:, :], ones_row[:, :], mrbfr[0:1, :], start=True, stop=True),
                 reads=[b_mrbf, b_consts], writes=[pR])
            P.op("ve", lambda o, c=cs: o.tensor_copy(mrrep[:, c], pb[2][:, :]),
                 reads=[pR], writes=[b_mrrep])

        for kc in range(NKC):
            P.op("ve", lambda o, k=kc: o.scalar_tensor_tensor(
                ttile[:, :], xT[k][:, :], ca(k), rrep[:, :], op0=ALU.mult, op1=ALU.mult),
                reads=[b_xT[kc], b_modfm, b_rrep], writes=[b_tt], sync=True)
            P.op("ve", lambda o, k=kc: o.scalar_tensor_tensor(
                ttile[:, :], mrrep[:, :], ca(k), ttile[:, :], op0=ALU.mult, op1=ALU.subtract),
                reads=[b_mrrep, b_modfm, b_tt], writes=[b_tt])
            P.op("ve", lambda o, k=kc: o.tensor_scalar(
                hT[k][:, :], ttile[:, :], -1.0, sa(k), op0=ALU.mult, op1=ALU.add),
                reads=[b_tt, b_modfm], writes=[b_hT[kc]], sync=True)
        P.end_scope()

    def attention():
        P.begin_scope()
        AR.reset()
        cosT = AR.alloc([128, S], BF16); b_cos = B("cos")
        sinT = AR.alloc([128, S], BF16); b_sin = B("sin")
        wqb = [AR.alloc([128, 8 * 128], BF16) for _ in range(3)]
        wkb = [AR.alloc([128, 8 * 128], BF16) for _ in range(3)]
        wvb = [AR.alloc([128, 8 * 128], BF16) for _ in range(3)]
        b_wqb = [B(f"wqb{i}") for i in range(3)]
        b_wkb = [B(f"wkb{i}") for i in range(3)]
        b_wvb = [B(f"wvb{i}") for i in range(3)]
        qpre = AR.alloc([128, S], BF16); b_qpre = [B(f"qpre{i}") for i in range(4)]
        kpre = AR.alloc([128, S], BF16); b_kpre = [B(f"kpre{i}") for i in range(4)]
        qfin = [AR.alloc([128, S], BF16) for _ in range(2)]
        kfin = [AR.alloc([128, S], BF16) for _ in range(2)]
        b_qfin = [[B(f"qfin{j}.{i}") for i in range(4)] for j in range(2)]
        b_kfin = [[B(f"kfin{j}.{i}") for i in range(4)] for j in range(2)]
        vbf = AR.alloc([128, S], F32); b_vbf = [B(f"vbf{i}") for i in range(4)]
        rop1 = AR.alloc([128, 512], F32); b_rop1 = B("rop1")
        rop2 = AR.alloc([128, 512], F32); b_rop2 = B("rop2")
        Vbuf = [AR.alloc([128, 16 * 130], BF16) for _ in range(2)]
        b_Vbuf = [[B(f"V{j}.{i}") for i in range(16)] for j in range(2)]
        NPT = 24
        PT = AR.alloc([128, NPT * 512], BF16)
        pt_off = AR.last_off
        b_PT = [B(f"PT{i}") for i in range(NPT)]
        sqq = nc.alloc_sbuf_tensor_at(f"sqq{AR.n}", [128, S], BF16, offset=pt_off)
        sqk = nc.alloc_sbuf_tensor_at(f"sqk{AR.n}", [128, S], BF16, offset=pt_off + 4096)
        b_sqq = [b_PT[0], b_PT[1], b_PT[2], b_PT[3]]
        b_sqk = [b_PT[4], b_PT[5], b_PT[6], b_PT[7]]
        recrow = AR.alloc([128, 512], F32); b_rec = B("rec")
        recb = AR.alloc([128, 512], BF16); b_recb = B("recb")
        lrep = AR.alloc([128, 512], F32); b_lrep = B("lrep")
        ctmp = AR.alloc([128, 512], F32); b_ctmp = B("ctmp")
        ctxBhi = AR.alloc([128, 512], F32); b_ctxBhi = B("ctxBhi")

        if not hasattr(AR, "_rep"):
            AR._rep = True
        P.op("dq", lambda o: o.dma_start(cosT[:, :], d_cos[:, :]), writes=[b_cos])
        P.op("dq", lambda o: o.dma_start(sinT[:, :], d_sin[:, :]), writes=[b_sin])
        for j in range(2):
            for t in range(16):
                P.op("ve", lambda o, j=j, t=t: o.memset(Vbuf[j][:, t * 130 + 64:t * 130 + 65], 1.0),
                     writes=[b_Vbuf[j][t]])
                P.op("ve", lambda o, j=j, t=t: o.memset(Vbuf[j][:, t * 130 + 129:t * 130 + 130], 1.0),
                     writes=[b_Vbuf[j][t]])

        def pair_weights(fc):
            for (w, tb, bb, nb) in ((wq, wqb, b_wqb, 3), (wk, wkb, b_wkb, 3), (wv, wvb, b_wvb, 3)):
                P.op("dg", lambda o, w=w, t=tb[fc % nb], f=fc: o.dma_start(
                    t[:, :].rearrange("p (kc m) -> p kc m", kc=8),
                    w[:, f * 128:(f + 1) * 128].rearrange("(kc p) m -> p kc m", p=128)),
                    writes=[bb[fc % nb]], tag=f"wqkv{fc}")

        def prep_pair(fc):
            pr = fc % 2
            wr = fc % 3
            if fc == 0:
                pair_weights(0)
                pair_weights(1)
            if fc < 6:
                pair_weights(fc + 2)

            def qkv_mm(tb, bb, ncx, bk, nb=3):
                for kc in range(NKC):
                    P.op("te", lambda o, t=tb[fc % nb], k=kc, n=ncx, b=bk: o.matmul(
                        pb[b][:, :], t[:, k * 128:(k + 1) * 128], hT[k][:, n * 512:(n + 1) * 512],
                        start=(k == 0), stop=(k == NKC - 1)),
                        reads=[bb[fc % nb], b_hT[kc]], writes=[b_pb[bk]], tag=f"qkv{fc}.{ncx}")

            def rope(pre, preb, fin, finb, ncx, bk):
                cs = slice(ncx * 512, (ncx + 1) * 512)
                P.op("te", lambda o, p=pre, c=cs, b=bk: o.matmul(
                    pb[b][:, :], prot[:, :], p[:, c], start=True, stop=True),
                    reads=[preb[ncx], b_consts], writes=[b_pb[bk]], tag=f"rot{fc}")
                P.op("ve", lambda o, c=cs, b=bk: o.tensor_tensor(rop1[:, :], pb[b][:, :], sinT[:, c], op=ALU.mult),
                     reads=[b_pb[bk], b_sin], writes=[b_rop1])
                P.op("ve", lambda o, p=pre, c=cs: o.tensor_tensor(rop2[:, :], p[:, c], cosT[:, c], op=ALU.mult),
                     reads=[preb[ncx], b_cos], writes=[b_rop2])
                P.op("ve", lambda o, f=fin, c=cs: o.tensor_tensor(f[:, c], rop1[:, :], rop2[:, :], op=ALU.add),
                     reads=[b_rop1, b_rop2], writes=[finb[ncx]])

            for ncx in range(4):
                qkv_mm(wqb, b_wqb, ncx, ncx % 4)
                P.op("ve", lambda o, n=ncx, f=fc, b=ncx % 4: o.tensor_scalar(
                    qpre[:, n * 512:(n + 1) * 512], pb[b][:, :], bqkv[:, f:f + 1], None, op0=ALU.add),
                    reads=[b_pb[ncx % 4], b_bqkv], writes=[b_qpre[ncx]])
                yield
            P.op("se", lambda o: o.activation(sqq[:, :], qpre[:, :], AF.Square),
                 reads=b_qpre, writes=b_sqq)
            for ncx in range(4):
                rope(qpre, b_qpre, qfin[pr], b_qfin[pr], ncx, 2 + ncx % 2)
                yield
            for ncx in range(4):
                P.op("te", lambda o, n=ncx, b=ncx % 4: o.matmul(
                    pb[b][0:1, :], ones_col[:, :], sqq[:, n * 512:(n + 1) * 512],
                    start=True, stop=True),
                    reads=b_sqq + [b_consts], writes=[b_pb[ncx % 4]], tag=f"qn{fc}")
                P.op("ve", lambda o, n=ncx, b=ncx % 4: o.reduce_max(
                    qnt[0:1, n:n + 1], pb[b][0:1, :], axis=AX.X),
                    reads=[b_pb[ncx % 4]], writes=[b_qnt], sync=True)
            yield
            for ncx in range(4):
                qkv_mm(wkb, b_wkb, ncx, ncx % 4)
                P.op("ve", lambda o, n=ncx, f=fc, b=ncx % 4: o.tensor_scalar(
                    kpre[:, n * 512:(n + 1) * 512], pb[b][:, :], bqkv[:, 8 + f:9 + f], None, op0=ALU.add),
                    reads=[b_pb[ncx % 4], b_bqkv], writes=[b_kpre[ncx]])
                yield
            P.op("se", lambda o: o.activation(sqk[:, :], kpre[:, :], AF.Square),
                 reads=b_kpre, writes=b_sqk)
            for ncx in range(4):
                rope(kpre, b_kpre, kfin[pr], b_kfin[pr], ncx, 2 + ncx % 2)
                yield
            for ncx in range(4):
                P.op("te", lambda o, n=ncx, b=ncx % 4: o.matmul(
                    pb[b][0:1, :], ones_col[:, :], sqk[:, n * 512:(n + 1) * 512],
                    start=True, stop=True),
                    reads=b_sqk + [b_consts], writes=[b_pb[ncx % 4]], tag=f"qn{fc}")
                P.op("ve", lambda o, n=ncx, b=ncx % 4: o.reduce_max(
                    qnt[0:1, 4 + n:5 + n], pb[b][0:1, :], axis=AX.X),
                    reads=[b_pb[ncx % 4]], writes=[b_qnt], sync=True)
            yield
            P.op("ve", lambda o: o.reduce_max(qnt[0:1, 8:9], qnt[0:1, 0:4], axis=AX.X),
                 reads=[b_qnt], writes=[b_qnt], sync=True)
            P.op("ve", lambda o: o.reduce_max(qnt[0:1, 9:10], qnt[0:1, 4:8], axis=AX.X),
                 reads=[b_qnt], writes=[b_qnt], sync=True)
            P.op("ve", lambda o: o.tensor_tensor(qnt[0:1, 10:11], qnt[0:1, 8:9], qnt[0:1, 9:10], op=ALU.mult),
                 reads=[b_qnt], writes=[b_qnt], sync=True)
            P.op("se", lambda o: o.activation(qnt[0:1, 13:14], qnt[0:1, 10:11], AF.Ln),
                 reads=[b_qnt], writes=[b_qnt])
            P.op("se", lambda o: o.activation(qnt[0:1, 11:12], qnt[0:1, 13:14], AF.Exp, scale=0.5),
                 reads=[b_qnt], writes=[b_qnt])
            P.op("ve", lambda o: o.tensor_scalar(qnt[0:1, 12:13], qnt[0:1, 11:12], -0.125, None, op0=ALU.mult),
                 reads=[b_qnt], writes=[b_qnt], sync=True)
            P.op("ve", lambda o: o.tensor_copy(qnb[0:1, 0:1], qnt[0:1, 12:13]),
                 reads=[b_qnt], writes=[b_qnb], sync=True)
            for ncx in range(4):
                qkv_mm(wvb, b_wvb, ncx, ncx % 4, nb=3)
                P.op("ve", lambda o, n=ncx, f=fc, b=ncx % 4: o.tensor_scalar(
                    vbf[:, n * 512:(n + 1) * 512], pb[b][:, :], bqkv[:, 16 + f:17 + f],
                    modfm[:, 2 * 8 + f:2 * 8 + f + 1], op0=ALU.add, op1=ALU.mult),
                    reads=[b_pb[ncx % 4], b_bqkv, b_modfm], writes=[b_vbf[ncx]], sync=True)
                yield
            P.op("te", lambda o: o.matmul(pb[2][:, 0:1], ones_row[:, :], qnb[0:1, 0:1], start=True, stop=True),
                 reads=[b_qnb, b_consts], writes=[b_pb[2]])
            P.op("ve", lambda o, pr=pr: o.tensor_copy(expb[:, pr:pr + 1], pb[2][:, 0:1]),
                 reads=[b_pb[2]], writes=[b_expb], sync=True)
            for t in range(16):
                vb_ = 2 + (t % 2)
                P.op("te", lambda o, t=t, b=vb_: o.transpose(
                    pb[b][:, 0:128], vbf[:, t * 128:(t + 1) * 128], identF[:, :]),
                    reads=[b_vbf[t // 4], b_consts], writes=[b_pb[vb_]], tag=f"vT{fc}.{t}")
                P.op("ve", lambda o, t=t, pr=pr, b=vb_: o.tensor_copy(Vbuf[pr][:, t * 130:t * 130 + 64], pb[b][:, 0:64]),
                     reads=[b_pb[vb_]], writes=[b_Vbuf[pr][t]])
                P.op("ve", lambda o, t=t, pr=pr, b=vb_: o.tensor_copy(Vbuf[pr][:, t * 130 + 65:t * 130 + 129], pb[b][:, 64:128]),
                     reads=[b_pb[vb_]], writes=[b_Vbuf[pr][t]])
                if t % 4 == 3:
                    yield

        blk_ctr = [0]
        pend_ctx = [None]
        pend_epi = [None]

        def emit_epilogue(st):
            fc, hh, qc, cb = st
            qs = slice(qc * 512, (qc + 1) * 512)
            P.op("se", lambda o, b=cb: o.activation(recrow[64:65, :], pb[b][64:65, :], AF.Ln,
                                                     bias=cvec[64:65, 1:2]),
                 reads=[b_pb[cb], b_cvec], writes=[b_rec])
            P.op("se", lambda o: o.activation(recb[64:65, :], recrow[64:65, :], AF.Exp, scale=-1.0),
                 reads=[b_rec], writes=[b_recb])
            P.op("te", lambda o: o.matmul(
                pb[6][0:64, :], ones_wide[64:65, 0:64], recb[64:65, :],
                start=True, stop=True),
                 reads=[b_recb, b_consts], writes=[b_pb[6]])
            P.op("ve", lambda o: o.tensor_copy(lrep[0:64, :], pb[6][0:64, :]),
                 reads=[b_pb[6]], writes=[b_lrep])
            P.op("ve", lambda o, b=cb: o.tensor_tensor(ctmp[0:64, :], pb[b][0:64, :], lrep[0:64, :], op=ALU.mult),
                 reads=[b_pb[cb], b_lrep], writes=[b_ctmp])
            if hh == 0:
                P.op("ve", lambda o, f=fc, q=qs: o.tensor_tensor(
                    xT[f][0:64, q], xT[f][0:64, q], ctmp[0:64, :], op=ALU.add),
                    reads=[b_ctmp, b_xT[fc]], writes=[b_xT[fc]], tag=f"xacc{fc}.{hh}.{qc}")
            else:
                P.op("dq", lambda o: o.dma_start(ctxBhi[64:128, :], ctmp[0:64, :]),
                     reads=[b_ctmp], writes=[b_ctxBhi], tag=f"cmv{fc}.{qc}")
                P.op("ve", lambda o, f=fc, q=qs: o.tensor_tensor(
                    xT[f][64:128, q], xT[f][64:128, q], ctxBhi[64:128, :], op=ALU.add),
                    reads=[b_ctxBhi, b_xT[fc]], writes=[b_xT[fc]], tag=f"xaccB{fc}.{qc}")

        def emit_ctx_step(kc):
            fc, hh, qc, cb, base = pend_ctx[0]
            pr = fc % 2
            vs = 65 * hh
            slot = (base + kc) % NPT
            P.op("te", lambda o, k=kc, b=cb, vs=vs, pr=pr, s=slot: o.matmul(
                pb[b][0:65, :], Vbuf[pr][:, k * 130 + vs:k * 130 + vs + 65],
                PT[:, s * 512:(s + 1) * 512],
                start=(k == 0), stop=(k == 15)),
                reads=[b_Vbuf[pr][kc], b_PT[slot]], writes=[b_pb[cb]], tag=f"ctx{fc}.{hh}.{qc}")

        def core_block(fc, hh, qc):
            pr = fc % 2
            off = hh * 64
            qs = slice(qc * 512, (qc + 1) * 512)
            base = (blk_ctr[0] * 16) % NPT
            for kc in range(16):
                bank = kc % 4
                P.op("te", lambda o, k=kc, of=off, q=qs, b=bank, pr=pr: o.matmul(
                    pb[b][:, :], kfin[pr][of:of + 64, k * 128:(k + 1) * 128], qfin[pr][of:of + 64, q],
                    start=True, stop=True),
                    reads=[b_kfin[pr][kc // 4], b_qfin[pr][qc], b_consts], writes=[b_pb[bank]],
                    tag=f"sc{fc}.{hh}.{qc}.{kc}")
                if kc % 2 == 1:
                    g = (kc % 4) // 2
                    s0 = (base + kc - 1) % NPT
                    P.op("se", lambda o, g=g, pr=pr, s0=s0: o.activation(
                        PT[:, s0 * 512:(s0 + 2) * 512], pwf[g], AF.Exp,
                        bias=expb[:, pr:pr + 1], scale=0.125),
                        reads=[b_pb[2 * g], b_pb[2 * g + 1], b_expb],
                        writes=[b_PT[s0], b_PT[s0 + 1]], tag=f"exp{fc}.{hh}.{qc}.{kc}")
                if pend_ctx[0] is not None:
                    emit_ctx_step(kc)
            if pend_ctx[0] is not None:
                if pend_epi[0] is not None:
                    emit_epilogue(pend_epi[0])
                pend_epi[0] = pend_ctx[0][:4]
            cb = 4 + (blk_ctr[0] % 2)
            blk_ctr[0] += 1
            pend_ctx[0] = (fc, hh, qc, cb, base)

        def flush_pending():
            if pend_ctx[0] is not None:
                for kc in range(16):
                    emit_ctx_step(kc)
                if pend_epi[0] is not None:
                    emit_epilogue(pend_epi[0])
                pend_epi[0] = pend_ctx[0][:4]
                pend_ctx[0] = None

        for fc in range(8):
            flush_pending()
            for _ in prep_pair(fc):
                pass
            for hh in range(2):
                for qc in range(4):
                    core_block(fc, hh, qc)
        flush_pending()
        emit_epilogue(pend_epi[0])
        P.end_scope()

    def ffn():
        P.begin_scope()
        AR.reset()
        h1s = AR.alloc([128, NHF * 2048], BF16)
        b_h1s = [B(f"h1s{m}") for m in range(NHF)]
        wrep = AR.alloc([128, S], F32); b_wrep = B("wrep")
        logt = AR.alloc([8, S], F32); b_logt = B("logt")
        wsb = AR.alloc([128, 128], F32); b_wsb = [B(f"w{t}") for t in range(16)]
        waug = AR.alloc([9, S], BF16); b_waug = B("waug")
        wrow = AR.alloc([1, S], BF16); b_wrow = B("wrow")
        rb2aug = AR.alloc([9, D], BF16); b_rb2aug = B("rb2aug")
        rw1b = [AR.alloc([128, 8 * 128], BF16) for _ in range(3)]
        b_rw1b = [B("rw1b0"), B("rw1b1"), B("rw1b2")]
        rw2b = [AR.alloc([128, 16 * 128], BF16) for _ in range(2)]
        b_rw2b = [B("rw2b0"), B("rw2b1")]
        ctF = AR.alloc([128, 512], F32); b_ctF = B("ctF")

        P.op("dg", lambda o: o.dma_start(rb2aug[0:8, :], rb2[:, :]), writes=[b_rb2aug])
        P.op("dg", lambda o: o.dma_start(rb2aug[8:9, :], sb2.rearrange("(one n) -> one n", one=1)),
             writes=[b_rb2aug])

        for ncx in range(4):
            cs = slice(ncx * 512, (ncx + 1) * 512)
            for kc in range(NKC):
                P.op("te", lambda o, k=kc, c=cs: o.matmul(
                    pb[0][0:8, :], rwrout[:, k * 8:(k + 1) * 8], hT[k][:, c],
                    start=(k == 0), stop=(k == NKC - 1)),
                    reads=[b_rwrout, b_hT[kc]], writes=[b_pb[0]], tag=f"rout{ncx}")
            P.op("ve", lambda o, c=cs: o.tensor_scalar(
                logt[0:8, c], pb[0][0:8, :], rbrow[0:8, 0:1], None, op0=ALU.add),
                reads=[b_pb[0], b_rbrow], writes=[b_logt])
        for t in range(16):
            P.op("te", lambda o, t=t: o.transpose(
                pb[1][:, 0:8], logt[0:8, t * 128:(t + 1) * 128], identF[0:8, 0:8]),
                reads=[b_logt, b_consts], writes=[b_pb[1]], tag=f"logT{t}")
            P.op("ve", lambda o: o.reduce_max(smscr[:, 0:1], pb[1][:, 0:8], axis=AX.X, negate=True),
                 reads=[b_pb[1]], writes=[b_smscr], sync=True)
            P.op("se", lambda o: o.activation(sme[:, :], pb[1][:, 0:8], AF.Exp, bias=smscr[:, 0:1]),
                 reads=[b_pb[1], b_smscr], writes=[b_sme])
            P.op("ve", lambda o: o.reduce_sum(smscr[:, 1:2], sme[:, :], axis=AX.X),
                 reads=[b_sme], writes=[b_smscr], sync=True)
            P.op("ve", lambda o: o.reduce_max(smscr[:, 2:3], sme[:, :], axis=AX.X),
                 reads=[b_sme], writes=[b_smscr], sync=True)
            P.op("ve", lambda o: o.tensor_scalar(smg[:, :], sme[:, :], smscr[:, 2:3], None, op0=ALU.is_ge),
                 reads=[b_sme, b_smscr], writes=[b_smg], sync=True)
            P.op("ve", lambda o: o.scalar_tensor_tensor(smg[:, :], smg[:, :], -2.0, sme[:, :],
                                                        op0=ALU.mult, op1=ALU.add),
                 reads=[b_sme], writes=[b_smg], sync=True)
            P.op("ve", lambda o: o.reduce_max(smscr[:, 3:4], smg[:, :], axis=AX.X),
                 reads=[b_smg], writes=[b_smscr], sync=True)
            P.op("se", lambda o: o.activation(smscr[:, 5:6], smscr[:, 1:2], AF.Ln),
                 reads=[b_smscr], writes=[b_smscr])
            P.op("se", lambda o: o.activation(smscr[:, 4:5], smscr[:, 5:6], AF.Exp, scale=-1.0),
                 reads=[b_smscr], writes=[b_smscr])
            P.op("ve", lambda o: o.tensor_scalar(smg[:, :], sme[:, :], smscr[:, 3:4], None, op0=ALU.is_ge),
                 reads=[b_sme, b_smscr], writes=[b_smg], sync=True)
            P.op("ve", lambda o, t=t: o.scalar_tensor_tensor(
                wsb[:, t * 8:(t + 1) * 8], sme[:, :], smscr[:, 4:5], smg[:, :],
                op0=ALU.mult, op1=ALU.mult),
                reads=[b_sme, b_smscr, b_smg], writes=[b_wsb[t]], tag=f"w{t}", sync=True)
        P.op("dq", lambda o: o.dma_start(waug[8:9, :], d_orow2k[:, :]), writes=[b_waug])
        for t in range(16):
            P.op("te", lambda o, t=t: o.transpose(
                pb[1][0:8, 0:128], wsb[:, t * 8:(t + 1) * 8], identF[:, :]),
                reads=[b_wsb[t], b_consts], writes=[b_pb[1]], tag=f"wT{t}")
            P.op("ve", lambda o, t=t: o.tensor_copy(waug[0:8, t * 128:(t + 1) * 128], pb[1][0:8, 0:128]),
                 reads=[b_pb[1]], writes=[b_waug])
        for dc in range(NKC):
            for tokc in range(4):
                cs = slice(tokc * 512, (tokc + 1) * 512)
                P.op("te", lambda o, d=dc, c=cs: o.matmul(
                    pb[0][:, :], rb2aug[0:9, d * 128:(d + 1) * 128], waug[0:9, c],
                    start=True, stop=True),
                    reads=[b_rb2aug, b_waug], writes=[b_pb[0]], tag=f"bm{dc}.{tokc}")
                P.op("ve", lambda o, d=dc, c=cs: o.scalar_tensor_tensor(
                    xT[d][:, c], pb[0][:, :], modfm[:, 5 * 8 + d:5 * 8 + d + 1], xT[d][:, c],
                    op0=ALU.mult, op1=ALU.add),
                    reads=[b_pb[0], b_modfm], writes=[b_xT[dc]], tag=f"bme{dc}.{tokc}", sync=True)

        def expert(e):
            shared = (e == -1)
            ei = max(e, 0)
            pw1 = (lambda mc: sw1[:, mc * 128:(mc + 1) * 128]) if shared else \
                  (lambda mc: rw1[ei, :, mc * 128:(mc + 1) * 128])
            pw2 = (lambda dc: sw2[:, dc * 128:(dc + 1) * 128]) if shared else \
                  (lambda dc: rw2[ei, :, dc * 128:(dc + 1) * 128])
            if shared:
                biasc = lambda mc: sb1fm[:, mc:mc + 1]
                b_bias = b_sb1fm
            else:
                half = e % 2
                P.op("dq", lambda o: o.dma_start(stage16[:, :], rb1[ei, :].rearrange("(j p) -> j p", p=128)),
                     writes=[b_stage16], tag=f"rb1{e}")
                P.op("te", lambda o: o.transpose(pb[1][:, 0:16], stage16[:, :], identF[0:16, 0:16]),
                     reads=[b_stage16, b_consts], writes=[b_pb[1]])
                P.op("ve", lambda o, h=half: o.tensor_copy(rb1fm[:, h * 16:(h + 1) * 16], pb[1][:, 0:16]),
                     reads=[b_pb[1]], writes=[b_rb1fm[half]])
                biasc = lambda mc, h=half: rb1fm[:, h * 16 + mc:h * 16 + mc + 1]
                b_bias = b_rb1fm[e % 2]
                P.op("dq", lambda o, e=e: o.dma_start(wrow[0:1, :], waug[e:e + 1, :]),
                     reads=[b_waug], writes=[b_wrow], tag=f"wrow{e}")
                for ncx in range(4):
                    cs = slice(ncx * 512, (ncx + 1) * 512)
                    P.op("te", lambda o, c=cs: o.matmul(pb[0][:, :], ones_row[:, :], wrow[0:1, c],
                                                        start=True, stop=True),
                         reads=[b_wrow, b_consts], writes=[b_pb[0]])
                    P.op("ve", lambda o, c=cs: o.tensor_copy(wrep[:, c], pb[0][:, :]),
                         reads=[b_pb[0]], writes=[b_wrep])
            for mc in range(NHF):
                wb = b_rw1b[mc % 3]
                P.op("dg", lambda o, m=mc: o.dma_start(
                    rw1b[m % 3][:, :].rearrange("p (kc m2) -> p kc m2", kc=8),
                    pw1(m).rearrange("(kc p) m2 -> p kc m2", p=128)),
                    writes=[wb], tag=f"rw1.{e}.{mc}")
                for tokc in range(4):
                    bank = 4 + (tokc % 2)
                    cs = slice(tokc * 512, (tokc + 1) * 512)
                    for kc in range(NKC):
                        P.op("te", lambda o, m=mc, k=kc, c=cs, b=bank: o.matmul(
                            pb[b][:, :], rw1b[m % 3][:, k * 128:(k + 1) * 128], hT[k][:, c],
                            start=(k == 0), stop=(k == NKC - 1)),
                            reads=[wb, b_hT[kc]], writes=[b_pb[bank]], tag=f"h1.{e}.{mc}.{tokc}")
                    P.op("se", lambda o, m=mc, c=cs, b=bank: o.activation(
                        h1s[:, m * 2048 + c.start:m * 2048 + c.stop], pb[b][:, :], AF.Gelu,
                        bias=biasc(m)),
                        reads=[b_pb[bank], b_bias], writes=[b_h1s[mc]], tag=f"gel.{e}.{mc}.{tokc}")
            for dc in range(NKC):
                wb = b_rw2b[dc % 2]
                P.op("dg", lambda o, d=dc: o.dma_start(
                    rw2b[d % 2][:, :].rearrange("p (kc m2) -> p kc m2", kc=16),
                    pw2(d).rearrange("(kc p) m2 -> p kc m2", p=128)),
                    writes=[wb], tag=f"rw2.{e}.{dc}")
                for tokc in range(4):
                    bank = 6 + (tokc % 2)
                    cs = slice(tokc * 512, (tokc + 1) * 512)
                    for kc in range(16):
                        P.op("te", lambda o, d=dc, k=kc, c=cs, b=bank: o.matmul(
                            pb[b][:, :], rw2b[d % 2][:, k * 128:(k + 1) * 128],
                            h1s[:, k * 2048 + c.start:k * 2048 + c.stop],
                            start=(k == 0), stop=(k == 15)),
                            reads=[wb, b_h1s[kc]], writes=[b_pb[bank]], tag=f"o2.{e}.{dc}.{tokc}")
                    if shared:
                        P.op("ve", lambda o, d=dc, c=cs, b=bank: o.scalar_tensor_tensor(
                            xT[d][:, c], pb[b][:, :], modfm[:, 5 * 8 + d:5 * 8 + d + 1], xT[d][:, c],
                            op0=ALU.mult, op1=ALU.add),
                            reads=[b_pb[bank], b_modfm], writes=[b_xT[dc]], tag=f"oacc.{e}.{dc}.{tokc}", sync=True)
                    else:
                        P.op("ve", lambda o, d=dc, c=cs, b=bank: o.scalar_tensor_tensor(
                            ctF[:, :], pb[b][:, :], modfm[:, 5 * 8 + d:5 * 8 + d + 1], wrep[:, c],
                            op0=ALU.mult, op1=ALU.mult),
                            reads=[b_pb[bank], b_modfm, b_wrep], writes=[b_ctF], tag=f"osc.{e}.{dc}.{tokc}", sync=True)
                        P.op("ve", lambda o, d=dc, c=cs: o.tensor_tensor(
                            xT[d][:, c], xT[d][:, c], ctF[:, :], op=ALU.add),
                            reads=[b_ctF, b_xT[dc]], writes=[b_xT[dc]], tag=f"oacc.{e}.{dc}.{tokc}")

        expert(-1)
        for e in range(E):
            expert(e)
        P.end_scope()

    def store():
        P.begin_scope()
        AR.reset()
        xo = [AR.alloc([128, D], F32) for _ in range(2)]
        b_xo = [B("xo0"), B("xo1")]
        for t in range(NTT):
            bo = b_xo[t % 2]
            for kc in range(NKC):
                bank = 6 + (kc % 2)
                P.op("te", lambda o, t=t, k=kc, b=bank: o.transpose(
                    pb[b][:, 0:128], xT[k][:, t * 128:(t + 1) * 128], identF[:, :]),
                    reads=[b_xT[kc], b_consts], writes=[b_pb[bank]], tag=f"xoT{t}.{kc}")
                P.op("ve", lambda o, t=t, k=kc, b=bank: o.tensor_copy(
                    xo[t % 2][:, k * 128:(k + 1) * 128], pb[b][:, 0:128]),
                    reads=[b_pb[bank]], writes=[bo], tag=f"xoe{t}.{kc}")
            P.op("dq", lambda o, t=t: o.dma_start(x_out[t * 128:(t + 1) * 128, :], xo[t % 2][:, :]),
                 reads=[bo], writes=[b_out], tag=f"st{t}")
        P.end_scope()
        P.op("se", lambda o: o.nop(), reads=[b_out], tag="fence")

    reg("xT", xT, b_xT)
    reg("hT", hT, b_hT)
    reg("modfm", modfm, b_modfm)

    for _layer in range(DEPTH):
        if stop_after == "load":
            break
        layer_norm(1)
        if stop_after == "ln1":
            break
        attention()
        if stop_after == "attn":
            break
        layer_norm(2)
        if stop_after == "ln2":
            break
        ffn()
    store()

    tap_outs = []
    for name in taps:
        tens, bufs = tap_reg[name]
        if isinstance(tens, list):
            for i, tt in enumerate(tens):
                od = nc.dram_tensor(f"tap_{name}_{i}", list(tt.shape), tt.dtype, kind="ExternalOutput")
                P.op("dq", lambda o, od=od, tt=tt: o.dma_start(od[:, :], tt[:, :]),
                     reads=[bufs[i]] if isinstance(bufs, list) else [bufs], writes=[b_out])
                tap_outs.append(f"tap_{name}_{i}")
        else:
            od = nc.dram_tensor(f"tap_{name}", list(tens.shape), tens.dtype, kind="ExternalOutput")
            P.op("dq", lambda o, od=od, tt=tens: o.dma_start(od[:, :], tt[:, :]),
                 reads=[bufs], writes=[b_out])
            tap_outs.append(f"tap_{name}")
    if taps:
        P.op("se", lambda o: o.nop(), reads=[b_out], tag="fence2")

    P.finalize()
    with nc.Block() as blk:
        @blk.tensor
        def _(o):
            P.emit("te", o, sems)

        @blk.vector
        def _(o):
            P.emit("ve", o, sems)

        @blk.scalar
        def _(o):
            P.emit("se", o, sems)

        @blk.gpsimd
        def _(o):
            P.emit("dg", o, sems)

        @blk.sync
        def _(o):
            P.emit("dq", o, sems)

    return nc, tap_outs


_IN_NAMES = dict(
    x="L0_attn_hidden_states", c="L0_c", wmod="L0_wmod",
    wq="L0_attn_Wq", bq="L0_attn_bq", wk="L0_attn_Wk", bk="L0_attn_bk",
    wv="L0_attn_Wv", bv="L0_attn_bv",
    router_w="L0_ffn_router_w", router_b="L0_ffn_router_b",
    sw1="L0_ffn_sw1", sb1="L0_ffn_sb1", sw2="L0_ffn_sw2", sb2="L0_ffn_sb2",
    rw1="L0_ffn_rw1", rb1="L0_ffn_rb1", rw2="L0_ffn_rw2", rb2="L0_ffn_rb2",
)


def make_in_maps(inputs):
    full = {k: np.ascontiguousarray(np.asarray(inputs[v], dtype=np.float32))
            for k, v in _IN_NAMES.items()}
    in_maps = []
    for b in range(8):
        m = dict(full)
        m["x"] = np.ascontiguousarray(full["x"][b])
        m["c"] = np.ascontiguousarray(full["c"][b])
        in_maps.append(m)
    return in_maps


def kernel(**inputs):
    nc, _ = build_program()
    in_maps = make_in_maps(inputs)
    res = run_bass_kernel_spmd(nc, in_maps, core_ids=list(range(8)))
    out = np.stack([np.asarray(r["out"], dtype=np.float32) for r in res.results], axis=0)
    return out


# version 17
# speedup vs baseline: 1.1102x; 1.1102x over previous
import numpy as np
import ml_dtypes
import concourse.bass as bass
import concourse.mybir as mybir
from concourse.bass_utils import run_bass_kernel_spmd

F32 = mybir.dt.float32
BF16 = mybir.dt.bfloat16
AF = mybir.ActivationFunctionType
ALU = mybir.AluOpType
AX = mybir.AxisListType

D = 1024
S = 2048
H = 16
HD = 64
E = 8
HF = 2048
NKC = 8
NTT = 16
NQC = 4
NHF = 16
EPS = 1e-5
DEPTH = 1

ENGS = ("te", "ve", "se", "dg", "dq")
STEP = {"te": 1, "ve": 1, "se": 1, "dg": 16, "dq": 16}


class Buf:
    __slots__ = ("w", "r", "name")

    def __init__(self, name=""):
        self.w = {}
        self.r = {}
        self.name = name


class Planner:
    def __init__(self):
        self.ops = {e: [] for e in ENGS}
        self.ghost = {}
        self.scope_bufs = None

    QUEUES = ("dg", "dq")

    def buf(self, name=""):
        b = Buf(name)
        if self.ghost:
            b.r = dict(self.ghost)
        if self.scope_bufs is not None:
            self.scope_bufs.append(b)
        return b

    def begin_scope(self):
        assert self.scope_bufs is None
        self.scope_bufs = []

    def end_scope(self):
        mx = dict(self.ghost)
        for b in self.scope_bufs:
            for e, i in list(b.r.items()) + list(b.w.items()):
                if i > mx.get(e, -1):
                    mx[e] = i
        self.ghost = mx
        self.scope_bufs = None

    def op(self, eng, emit, reads=(), writes=(), tag="", sync=False):
        idx = len(self.ops[eng])
        deps = {}
        inq = eng in self.QUEUES

        def add(e, i, force=False):
            if (e != eng or force or e in self.QUEUES) and i > deps.get(e, -1):
                deps[e] = i

        for b in reads:
            for e, i in b.w.items():
                add(e, i, force=True)
        for b in writes:
            for e, i in b.w.items():
                add(e, i, force=sync)
            for e, i in b.r.items():
                add(e, i, force=True)
        self.ops[eng].append((emit, deps, tag))
        for b in reads:
            if b.r.get(eng, -1) < idx:
                b.r[eng] = idx
        for b in writes:
            b.w[eng] = idx
            b.r = {}
            if inq or len(b.w) > 1:
                pass
            else:
                b.w = {eng: idx}
        return (eng, idx)

    def check_deadlock(self):
        ptr = {e: 0 for e in ENGS}
        total = sum(len(self.ops[e]) for e in ENGS)
        done = 0
        progress = True
        while progress:
            progress = False
            for e in ENGS:
                while ptr[e] < len(self.ops[e]):
                    _, deps, tag = self.ops[e][ptr[e]]
                    if all(ptr[de] > di for de, di in deps.items()):
                        ptr[e] += 1
                        done += 1
                        progress = True
                    else:
                        break
        if done != total:
            stuck = {e: (ptr[e], len(self.ops[e]),
                         self.ops[e][ptr[e]][2] if ptr[e] < len(self.ops[e]) else "")
                     for e in ENGS}
            raise RuntimeError(f"planner deadlock: {stuck}")

    NSEM = 4

    def finalize(self):
        self.check_deadlock()
        incset = {e: set() for e in ENGS}
        for e in ENGS:
            for (_, deps, _) in self.ops[e]:
                for de, di in deps.items():
                    incset[de].add(di)
        self.rank = {}
        self.incset = {}
        for e in ("te", "ve", "se"):
            srt = sorted(incset[e])
            self.incset[e] = set(srt)
            r = {}
            for pos, i in enumerate(srt):
                r[i] = pos + 1
            self.rank[e] = r
        self.qsem = {}
        for e in ("dg", "dq"):
            n = len(self.ops[e])
            self.qsem[e] = [(i % self.NSEM, i // self.NSEM + 1) for i in range(n)]

    def dep_sem(self, e, i):
        if e in ("dg", "dq"):
            slot, k = self.qsem[e][i]
            return (f"{e}{slot}", 16 * k)
        return (e, self.rank[e][i])

    def emit(self, eng, obj, sems):
        waited = {}

        def wait(key, v):
            if v > waited.get(key, 0):
                obj.wait_ge(sems[key], v)
                waited[key] = v

        if eng in ("dg", "dq"):
            for idx, (emit_fn, deps, tag) in enumerate(self.ops[eng]):
                slot, k = self.qsem[eng][idx]
                if k > 1:
                    wait(f"{eng}{slot}", 16 * (k - 1))
                for de, di in deps.items():
                    wait(*self.dep_sem(de, di))
                inst = emit_fn(obj)
                inst.then_inc(sems[f"{eng}{slot}"], 16)
        else:
            inc = self.incset[eng]
            for idx, (emit_fn, deps, tag) in enumerate(self.ops[eng]):
                for de, di in deps.items():
                    wait(*self.dep_sem(de, di))
                inst = emit_fn(obj)
                if idx in inc:
                    inst.then_inc(sems[eng], 1)


class Arena:

    def __init__(self, nc, nbytes):
        r = nc.bump_sbuf(nbytes)
        assert r is not None, "arena alloc failed"
        self.nc = nc
        self.base = r[0]
        self.size = nbytes
        self.cur = 0
        self.n = 0

    def reset(self):
        self.cur = 0

    def alloc(self, shape, dtype):
        nb = int(np.prod(shape[1:])) * mybir.dt.size(dtype)
        nb = (nb + 31) & ~31
        assert self.cur + nb <= self.size, f"arena overflow {self.cur + nb} > {self.size}"
        self.n += 1
        self.last_off = self.base + self.cur
        t = self.nc.alloc_sbuf_tensor_at(f"ar{self.n}", list(shape), dtype,
                                         offset=self.base + self.cur)
        self.cur += nb
        return t


def _host_consts():
    pos = np.arange(S, dtype=np.float64)
    inv = np.power(10000.0, -2.0 * np.arange(HD // 2, dtype=np.float64) / HD)
    ang = pos[:, None] * inv[None, :]
    cos2 = np.repeat(np.cos(ang), 2, axis=-1)
    sin2 = np.repeat(np.sin(ang), 2, axis=-1)
    cosT = np.empty((128, S), dtype=np.float32)
    sinT = np.empty((128, S), dtype=np.float32)
    cosT[0:64, :] = cos2.T
    cosT[64:128, :] = cos2.T
    sinT[0:64, :] = sin2.T
    sinT[64:128, :] = sin2.T
    prot = np.zeros((128, 128), dtype=np.float32)
    for blk in (0, 64):
        for i in range(32):
            prot[blk + 2 * i + 1, blk + 2 * i] = -1.0
            prot[blk + 2 * i, blk + 2 * i + 1] = 1.0
    return (cosT.astype(ml_dtypes.bfloat16), sinT.astype(ml_dtypes.bfloat16),
            prot.astype(ml_dtypes.bfloat16))


def build_program(taps=(), stop_after=None):
    nc = bass.Bass("TRN2", target_bir_lowering=False)
    P = Planner()

    dt = {}
    def din(name, shape):
        dt[name] = nc.dram_tensor(name, list(shape), F32, kind="ExternalInput")
        return dt[name]

    x_in = din("x", (S, D))
    c_in = din("c", (D,))
    wmod = din("wmod", (D, 6 * D))
    wq = din("wq", (D, D)); bq = din("bq", (D,))
    wk = din("wk", (D, D)); bk = din("bk", (D,))
    wv = din("wv", (D, D)); bv = din("bv", (D,))
    router_w = din("router_w", (D, E)); router_b = din("router_b", (E,))
    sw1 = din("sw1", (D, HF)); sb1 = din("sb1", (HF,))
    sw2 = din("sw2", (HF, D)); sb2 = din("sb2", (D,))
    rw1 = din("rw1", (E, D, HF)); rb1 = din("rb1", (E, HF))
    rw2 = din("rw2", (E, HF, D)); rb2 = din("rb2", (E, D))
    x_out = nc.dram_tensor("out", [S, D], F32, kind="ExternalOutput")

    cosT_np, sinT_np, prot_np = _host_consts()
    d_cos = nc.inline_tensor(cosT_np, "c_cos")
    d_sin = nc.inline_tensor(sinT_np, "c_sin")
    d_prot = nc.inline_tensor(prot_np, "c_prot")
    d_idf = nc.inline_tensor(np.eye(128, dtype=np.float32), "c_idf")
    d_idb = nc.inline_tensor(np.eye(128, dtype=ml_dtypes.bfloat16), "c_idb")
    d_or = nc.inline_tensor(np.ones((1, 128), dtype=ml_dtypes.bfloat16), "c_or")
    d_oc = nc.inline_tensor(np.ones((128, 1), dtype=ml_dtypes.bfloat16), "c_oc")
    d_orow2k = nc.inline_tensor(np.ones((1, S), dtype=ml_dtypes.bfloat16), "c_or2k")

    A = nc.alloc_sbuf_tensor
    xT = [A(f"xT{k}", [128, S], F32) for k in range(NKC)]
    hT = [A(f"hT{k}", [128, S], BF16) for k in range(NKC)]
    identF = A("identF", [128, 128], F32)
    identB = A("identB", [128, 128], BF16)
    prot = A("prot", [128, 128], BF16)
    ones_row = A("ones_row", [1, 128], BF16)
    ones_col = A("ones_col", [128, 1], BF16)
    modfm = A("modfm", [128, 6 * 8], F32)
    bqkv = A("bqkv", [128, 24], F32)
    sb1fm = A("sb1fm", [128, 16], F32)
    rb1fm = A("rb1fm", [128, 32], F32)
    rbrow = A("rbrow", [8, 1], F32)
    rwrout = A("rwrout", [128, 64], BF16)
    stage16 = A("stage16", [16, 128], F32)
    fmtmp = A("fmtmp", [128, 16], F32)
    smscr = A("smscr", [128, 8], F32)
    sme = A("sme", [128, 8], F32)
    smg = A("smg", [128, 8], F32)
    qnt = A("qnt", [1, 16], F32)
    qnb = A("qnb", [1, 2], BF16)
    expb = A("expb", [128, 2], F32)
    cvec = A("cvec", [128, 2], F32)
    ones_wide = A("ones_wide", [128, 64], BF16)

    pw = [nc.alloc_psum_tensor(f"pw{i}", [128, 1024], F32) for i in range(2)]
    pb = []
    for i in range(2):
        pb += [pw[i][:, 0:512], pw[i][:, 512:1024]]
    pb += [nc.alloc_psum_tensor(f"pb{i}", [128, 512], F32)[:, :] for i in range(4, 8)]
    pwf = [pw[0][:, :], pw[1][:, :]]

    AR = Arena(nc, 107 * 1024)

    sems = {e: nc.alloc_semaphore(f"sem_{e}") for e in ("te", "ve", "se")}
    for e in ("dg", "dq"):
        for s in range(Planner.NSEM):
            sems[f"{e}{s}"] = nc.alloc_semaphore(f"sem_{e}{s}")

    B = P.buf
    b_xT = [B(f"xT{k}") for k in range(NKC)]
    b_hT = [B(f"hT{k}") for k in range(NKC)]
    b_consts = B("consts")
    b_modfm = B("modfm")
    b_bqkv = B("bqkv")
    b_sb1fm = B("sb1fm")
    b_rb1fm = [B("rb1fm0"), B("rb1fm1")]
    b_rbrow = B("rbrow")
    b_rwrout = B("rwrout")
    b_stage16 = B("stage16")
    b_fmtmp = B("fmtmp")
    b_smscr = B("smscr"); b_sme = B("sme"); b_smg = B("smg")
    b_qnt = B("qnt"); b_qnb = B("qnb"); b_expb = B("expb")
    b_cvec = B("cvec")
    b_pb = [B(f"pb{i}") for i in range(8)]
    b_out = B("outdma")

    tap_reg = {}

    def reg(name, tensor, buf):
        tap_reg[name] = (tensor, buf)

    for dst, src in ((identF, d_idf), (identB, d_idb), (prot, d_prot),
                     (ones_row, d_or), (ones_col, d_oc)):
        P.op("dq", lambda o, d=dst, s=src: o.dma_start(d[:, :], s[:, :]),
             writes=[b_consts], tag="const")
    P.op("dq", lambda o: o.dma_start(rbrow[:, :], router_b.rearrange("(p one) -> p one", one=1)),
         writes=[b_rbrow])
    P.op("ve", lambda o: o.memset(cvec[:, 0:1], EPS), writes=[b_cvec])
    P.op("ve", lambda o: o.memset(cvec[:, 1:2], 1e-30), writes=[b_cvec])
    P.op("ve", lambda o: o.memset(ones_wide[:, :], 1.0), writes=[b_consts])
    P.op("dg", lambda o: o.dma_start(
        rwrout[:, :].rearrange("p (kc e) -> p kc e", kc=8),
        router_w.rearrange("(kc p) e -> p kc e", p=128)), writes=[b_rwrout])

    def load_fm(vec_dram, n, dst_ap_fn, bufw, tag):
        P.op("dq", lambda o: o.dma_start(stage16[0:n, :],
             vec_dram.rearrange("(j p) -> j p", p=128)),
             reads=[], writes=[b_stage16], tag=tag)
        P.op("te", lambda o: o.transpose(pb[7][:, 0:n], stage16[0:n, :], identF[0:n, 0:n]),
             reads=[b_stage16, b_consts], writes=[b_pb[7]], tag=tag)
        P.op("ve", lambda o: o.tensor_copy(dst_ap_fn(), pb[7][:, 0:n]),
             reads=[b_pb[7]], writes=[bufw], tag=tag)

    load_fm(bq, 8, lambda: bqkv[:, 0:8], b_bqkv, "bq")
    load_fm(bk, 8, lambda: bqkv[:, 8:16], b_bqkv, "bk")
    load_fm(bv, 8, lambda: bqkv[:, 16:24], b_bqkv, "bv")
    load_fm(sb1, 16, lambda: sb1fm[:, :], b_sb1fm, "sb1")

    P.begin_scope()
    AR.reset()
    xin = [AR.alloc([128, D], F32) for _ in range(3)]
    b_xin = [B(f"xin{i}") for i in range(3)]
    silu_fm = AR.alloc([128, 8], F32); b_silufm = B("silufm")
    silu_rep = AR.alloc([128, 8 * 128], F32); b_silurep = B("silurep")
    wstage = [AR.alloc([128, 8 * 512], F32) for _ in range(2)]
    b_wstage = [B("ws0"), B("ws1")]
    modsb = [AR.alloc([128, 512], F32) for _ in range(2)]
    b_modsb = [B("modsb0"), B("modsb1")]

    P.op("dq", lambda o: o.dma_start(stage16[0:8, :], c_in.rearrange("(j p) -> j p", p=128)),
         writes=[b_stage16])
    P.op("te", lambda o: o.transpose(pb[7][:, 0:8], stage16[0:8, :], identF[0:8, 0:8]),
         reads=[b_stage16, b_consts], writes=[b_pb[7]])
    P.op("se", lambda o: o.activation(silu_fm[:, :], pb[7][:, 0:8], AF.Silu),
         reads=[b_pb[7]], writes=[b_silufm])
    for kc in range(NKC):
        P.op("ve", lambda o, k=kc: o.tensor_scalar(
            silu_rep[:, k * 128:(k + 1) * 128], identF[:, :],
            0.0, silu_fm[:, k:k + 1], op0=ALU.mult, op1=ALU.add),
            reads=[b_silufm, b_consts], writes=[b_silurep])

    def emit_x_tile(t):
        bi = b_xin[t % 3]
        P.op("dq", lambda o, t=t: o.dma_start(xin[t % 3][:, :], x_in[t * 128:(t + 1) * 128, :]),
             writes=[bi], tag=f"xin{t}")
        for kc in range(NKC):
            pbank = b_pb[6]
            P.op("te", lambda o, t=t, k=kc: o.transpose(
                pb[6][:, 0:128], xin[t % 3][:, k * 128:(k + 1) * 128], identF[:, :]),
                reads=[bi, b_consts], writes=[pbank], tag=f"xT{t}.{kc}")
            P.op("se", lambda o, t=t, k=kc: o.copy(
                xT[k][:, t * 128:(t + 1) * 128], pb[6][:, 0:128]),
                reads=[pbank], writes=[b_xT[kc]], tag=f"xTe{t}.{kc}")

    def emit_mod_group(ncg):
        ws = b_wstage[ncg % 2]
        P.op("dg", lambda o, g=ncg: o.dma_start(
            wstage[g % 2][:, :].rearrange("p (kc n) -> p kc n", kc=8),
            wmod[:, g * 512:(g + 1) * 512].rearrange("(kc p) n -> p kc n", p=128)),
            writes=[ws], tag=f"wmod{ncg}")
        for kc in range(NKC):
            P.op("te", lambda o, g=ncg, k=kc: o.matmul(
                pb[5][:, :], silu_rep[:, k * 128:(k + 1) * 128],
                wstage[g % 2][:, k * 512:(k + 1) * 512],
                start=(k == 0), stop=(k == NKC - 1)),
                reads=[b_silurep, ws], writes=[b_pb[5]], tag=f"mod{ncg}")
        P.op("ve", lambda o, g=ncg: o.tensor_copy(modsb[g % 2][:, :], pb[5][:, :]),
             reads=[b_pb[5]], writes=[b_modsb[ncg % 2]], tag=f"modsb{ncg}")
        for j in range(4):
            f = ncg * 4 + j
            v = f // 8
            cc = f % 8
            P.op("te", lambda o, g=ncg, j=j: o.transpose(
                pb[4][:, 0:128], modsb[g % 2][:, j * 128:(j + 1) * 128], identF[:, :]),
                reads=[b_modsb[ncg % 2], b_consts], writes=[b_pb[4]], tag=f"modT{f}")
            addc = 1.0 if v in (1, 4) else 0.0
            P.op("ve", lambda o, v=v, cc=cc, a=addc: o.tensor_scalar(
                modfm[:, v * 8 + cc:v * 8 + cc + 1], pb[4][:, 0:1], a, None, op0=ALU.add),
                reads=[b_pb[4]], writes=[b_modfm], tag=f"modfm{f}")

    for i in range(12):
        emit_mod_group(i)
        emit_x_tile(i)
    for t in range(12, 16):
        emit_x_tile(t)

    for v in (2, 5):
        sl = modfm[:, v * 8:(v + 1) * 8]
        P.op("se", lambda o, sl=sl: o.activation(fmtmp[:, 0:8], sl, AF.Abs),
             reads=[b_modfm], writes=[b_fmtmp])
        P.op("ve", lambda o: o.tensor_scalar(fmtmp[:, 0:8], fmtmp[:, 0:8], 0.25, None, op0=ALU.max),
             writes=[b_fmtmp])
        P.op("ve", lambda o, sl=sl: o.tensor_scalar(
            fmtmp[:, 8:16], sl, 0.0, 2.0, op0=ALU.is_ge, op1=ALU.mult),
            reads=[b_modfm], writes=[b_fmtmp], sync=True)
        P.op("ve", lambda o, sl=sl: o.tensor_tensor(sl, fmtmp[:, 8:16], fmtmp[:, 0:8], op=ALU.mult),
             reads=[b_fmtmp], writes=[b_modfm], sync=True)
        P.op("ve", lambda o, sl=sl: o.tensor_tensor(sl, sl, fmtmp[:, 0:8], op=ALU.subtract),
             reads=[b_fmtmp], writes=[b_modfm], sync=True)
    P.end_scope()

    def layer_norm(which):
        P.begin_scope()
        AR.reset()
        xbf = [AR.alloc([128, 512], BF16) for _ in range(2)]
        sqbf = [AR.alloc([128, 512], BF16) for _ in range(2)]
        b_xbf = [B("xbf0"), B("xbf1")]
        b_sqbf = [B("sqbf0"), B("sqbf1")]
        m2row = AR.alloc([1, 512], F32); b_m2 = B("m2")
        srow = AR.alloc([1, 512], F32); b_sr = B("sr")
        verow = AR.alloc([1, 512], F32); b_ve_ = B("verow")
        r0row = AR.alloc([1, 512], F32); b_r0 = B("r0")
        t1row = AR.alloc([1, 512], F32); b_t1 = B("t1")
        rrow = AR.alloc([1, 512], F32); b_rr = B("rr")
        mrrow = AR.alloc([1, 512], F32); b_mr = B("mr")
        rbfr = AR.alloc([1, 512], BF16); b_rbf = B("rbf")
        mrbfr = AR.alloc([1, 512], BF16); b_mrbf = B("mrbf")
        rrep = AR.alloc([128, S], F32); b_rrep = B("rrep")
        mrrep = AR.alloc([128, S], F32); b_mrrep = B("mrrep")
        ttile = AR.alloc([128, S], F32); b_tt = B("ttile")

        pS, pQ, pR = b_pb[0], b_pb[1], b_pb[2]
        if which == 1:
            ca = lambda k: modfm[:, 1 * 8 + k:1 * 8 + k + 1]
            sa = lambda k: modfm[:, 0 * 8 + k:0 * 8 + k + 1]
        else:
            ca = lambda k: modfm[:, 4 * 8 + k:4 * 8 + k + 1]
            sa = lambda k: modfm[:, 3 * 8 + k:3 * 8 + k + 1]

        for ch in range(NQC):
            cs = slice(ch * 512, (ch + 1) * 512)
            for kc in range(NKC):
                P.op("se", lambda o, k=kc, c=cs: o.copy(xbf[k % 2][:, :], xT[k][:, c]),
                     reads=[b_xT[kc]], writes=[b_xbf[kc % 2]])
                P.op("se", lambda o, k=kc, c=cs: o.activation(sqbf[k % 2][:, :], xT[k][:, c], AF.Square),
                     reads=[b_xT[kc]], writes=[b_sqbf[kc % 2]])
                P.op("te", lambda o, k=kc: o.matmul(pb[0][0:1, :], ones_col[:, :], xbf[k % 2][:, :],
                     start=(k == 0), stop=(k == NKC - 1)),
                     reads=[b_xbf[kc % 2], b_consts], writes=[pS])
                P.op("te", lambda o, k=kc: o.matmul(pb[1][0:1, :], ones_col[:, :], sqbf[k % 2][:, :],
                     start=(k == 0), stop=(k == NKC - 1)),
                     reads=[b_sqbf[kc % 2], b_consts], writes=[pQ])
            P.op("ve", lambda o: o.tensor_scalar(pb[0][0:1, :], pb[0][0:1, :], 1.0 / D, None, op0=ALU.mult),
                 reads=[pS], writes=[pS])
            P.op("ve", lambda o: o.tensor_scalar(pb[1][0:1, :], pb[1][0:1, :], 1.0 / D, None, op0=ALU.mult),
                 reads=[pQ], writes=[pQ])
            P.op("se", lambda o: o.activation(m2row[0:1, :], pb[0][0:1, :], AF.Square),
                 reads=[pS], writes=[b_m2])
            P.op("ve", lambda o: o.tensor_tensor(pb[1][0:1, :], pb[1][0:1, :], m2row[0:1, :], op=ALU.subtract),
                 reads=[pQ, b_m2], writes=[pQ])
            P.op("se", lambda o: o.activation(srow[0:1, :], pb[1][0:1, :], AF.Ln, bias=cvec[0:1, 0:1]),
                 reads=[pQ, b_cvec], writes=[b_sr])
            P.op("se", lambda o: o.activation(rrow[0:1, :], srow[0:1, :], AF.Exp, scale=-0.5),
                 reads=[b_sr], writes=[b_rr])
            P.op("se", lambda o: o.activation(rbfr[0:1, :], srow[0:1, :], AF.Exp, scale=-0.5),
                 reads=[b_sr], writes=[b_rbf])
            P.op("ve", lambda o: o.tensor_tensor(mrrow[0:1, :], pb[0][0:1, :], rrow[0:1, :], op=ALU.mult),
                 reads=[pS, b_rr], writes=[b_mr])
            P.op("ve", lambda o: o.tensor_copy(mrbfr[0:1, :], mrrow[0:1, :]),
                 reads=[b_mr], writes=[b_mrbf])
            P.op("te", lambda o: o.matmul(pb[2][:, :], ones_row[:, :], rbfr[0:1, :], start=True, stop=True),
                 reads=[b_rbf, b_consts], writes=[pR])
            P.op("ve", lambda o, c=cs: o.tensor_copy(rrep[:, c], pb[2][:, :]),
                 reads=[pR], writes=[b_rrep])
            P.op("te", lambda o: o.matmul(pb[2][:, :], ones_row[:, :], mrbfr[0:1, :], start=True, stop=True),
                 reads=[b_mrbf, b_consts], writes=[pR])
            P.op("ve", lambda o, c=cs: o.tensor_copy(mrrep[:, c], pb[2][:, :]),
                 reads=[pR], writes=[b_mrrep])

        for kc in range(NKC):
            P.op("ve", lambda o, k=kc: o.scalar_tensor_tensor(
                ttile[:, :], xT[k][:, :], ca(k), rrep[:, :], op0=ALU.mult, op1=ALU.mult),
                reads=[b_xT[kc], b_modfm, b_rrep], writes=[b_tt], sync=True)
            P.op("ve", lambda o, k=kc: o.scalar_tensor_tensor(
                ttile[:, :], mrrep[:, :], ca(k), ttile[:, :], op0=ALU.mult, op1=ALU.subtract),
                reads=[b_mrrep, b_modfm, b_tt], writes=[b_tt])
            P.op("ve", lambda o, k=kc: o.tensor_scalar(
                hT[k][:, :], ttile[:, :], -1.0, sa(k), op0=ALU.mult, op1=ALU.add),
                reads=[b_tt, b_modfm], writes=[b_hT[kc]], sync=True)
        P.end_scope()

    def attention():
        P.begin_scope()
        AR.reset()
        cosT = AR.alloc([128, S], BF16); b_cos = B("cos")
        sinT = AR.alloc([128, S], BF16); b_sin = B("sin")
        wqb = [AR.alloc([128, 8 * 128], BF16) for _ in range(3)]
        wkb = [AR.alloc([128, 8 * 128], BF16) for _ in range(3)]
        wvb = [AR.alloc([128, 8 * 128], BF16) for _ in range(3)]
        b_wqb = [B(f"wqb{i}") for i in range(3)]
        b_wkb = [B(f"wkb{i}") for i in range(3)]
        b_wvb = [B(f"wvb{i}") for i in range(3)]
        qpre = AR.alloc([128, S], BF16); b_qpre = [B(f"qpre{i}") for i in range(4)]
        kpre = AR.alloc([128, S], BF16); b_kpre = [B(f"kpre{i}") for i in range(4)]
        qfin = [AR.alloc([128, S], BF16) for _ in range(2)]
        kfin = [AR.alloc([128, S], BF16) for _ in range(2)]
        b_qfin = [[B(f"qfin{j}.{i}") for i in range(4)] for j in range(2)]
        b_kfin = [[B(f"kfin{j}.{i}") for i in range(4)] for j in range(2)]
        vbf = AR.alloc([128, S], F32); b_vbf = [B(f"vbf{i}") for i in range(4)]
        rop1 = AR.alloc([128, 512], F32); b_rop1 = B("rop1")
        rop2 = AR.alloc([128, 512], F32); b_rop2 = B("rop2")
        Vbuf = [AR.alloc([128, 16 * 130], BF16) for _ in range(2)]
        b_Vbuf = [[B(f"V{j}.{i}") for i in range(16)] for j in range(2)]
        NPT = 24
        PT = AR.alloc([128, NPT * 512], BF16)
        pt_off = AR.last_off
        b_PT = [B(f"PT{i}") for i in range(NPT)]
        sqq = nc.alloc_sbuf_tensor_at(f"sqq{AR.n}", [128, S], BF16, offset=pt_off)
        sqk = nc.alloc_sbuf_tensor_at(f"sqk{AR.n}", [128, S], BF16, offset=pt_off + 4096)
        b_sqq = [b_PT[0], b_PT[1], b_PT[2], b_PT[3]]
        b_sqk = [b_PT[4], b_PT[5], b_PT[6], b_PT[7]]
        recrow = AR.alloc([128, 512], F32); b_rec = B("rec")
        recb = AR.alloc([128, 512], BF16); b_recb = B("recb")
        lrep = AR.alloc([128, 512], F32); b_lrep = B("lrep")
        ctmp = AR.alloc([128, 512], F32); b_ctmp = B("ctmp")
        ctxBhi = AR.alloc([128, 512], F32); b_ctxBhi = B("ctxBhi")

        if not hasattr(AR, "_rep"):
            AR._rep = True
        P.op("dq", lambda o: o.dma_start(cosT[:, :], d_cos[:, :]), writes=[b_cos])
        P.op("dq", lambda o: o.dma_start(sinT[:, :], d_sin[:, :]), writes=[b_sin])
        for j in range(2):
            for t in range(16):
                P.op("ve", lambda o, j=j, t=t: o.memset(Vbuf[j][:, t * 130 + 64:t * 130 + 65], 1.0),
                     writes=[b_Vbuf[j][t]])
                P.op("ve", lambda o, j=j, t=t: o.memset(Vbuf[j][:, t * 130 + 129:t * 130 + 130], 1.0),
                     writes=[b_Vbuf[j][t]])

        def pair_weights(fc):
            for (w, tb, bb, nb) in ((wq, wqb, b_wqb, 3), (wk, wkb, b_wkb, 3), (wv, wvb, b_wvb, 3)):
                P.op("dg", lambda o, w=w, t=tb[fc % nb], f=fc: o.dma_start(
                    t[:, :].rearrange("p (kc m) -> p kc m", kc=8),
                    w[:, f * 128:(f + 1) * 128].rearrange("(kc p) m -> p kc m", p=128)),
                    writes=[bb[fc % nb]], tag=f"wqkv{fc}")

        def prep_pair(fc):
            pr = fc % 2
            wr = fc % 3
            if fc == 0:
                pair_weights(0)
                pair_weights(1)
            if fc < 6:
                pair_weights(fc + 2)

            def qkv_mm(tb, bb, ncx, bk, nb=3):
                for kc in range(NKC):
                    P.op("te", lambda o, t=tb[fc % nb], k=kc, n=ncx, b=bk: o.matmul(
                        pb[b][:, :], t[:, k * 128:(k + 1) * 128], hT[k][:, n * 512:(n + 1) * 512],
                        start=(k == 0), stop=(k == NKC - 1)),
                        reads=[bb[fc % nb], b_hT[kc]], writes=[b_pb[bk]], tag=f"qkv{fc}.{ncx}")

            def rope(pre, preb, fin, finb, ncx, bk):
                cs = slice(ncx * 512, (ncx + 1) * 512)
                P.op("te", lambda o, p=pre, c=cs, b=bk: o.matmul(
                    pb[b][:, :], prot[:, :], p[:, c], start=True, stop=True),
                    reads=[preb[ncx], b_consts], writes=[b_pb[bk]], tag=f"rot{fc}")
                P.op("ve", lambda o, c=cs, b=bk: o.tensor_tensor(rop1[:, :], pb[b][:, :], sinT[:, c], op=ALU.mult),
                     reads=[b_pb[bk], b_sin], writes=[b_rop1])
                P.op("ve", lambda o, p=pre, c=cs: o.tensor_tensor(rop2[:, :], p[:, c], cosT[:, c], op=ALU.mult),
                     reads=[preb[ncx], b_cos], writes=[b_rop2])
                P.op("ve", lambda o, f=fin, c=cs: o.tensor_tensor(f[:, c], rop1[:, :], rop2[:, :], op=ALU.add),
                     reads=[b_rop1, b_rop2], writes=[finb[ncx]])

            for ncx in range(4):
                qkv_mm(wqb, b_wqb, ncx, ncx % 4)
                P.op("ve", lambda o, n=ncx, f=fc, b=ncx % 4: o.tensor_scalar(
                    qpre[:, n * 512:(n + 1) * 512], pb[b][:, :], bqkv[:, f:f + 1], None, op0=ALU.add),
                    reads=[b_pb[ncx % 4], b_bqkv], writes=[b_qpre[ncx]])
                yield
            P.op("se", lambda o: o.activation(sqq[:, :], qpre[:, :], AF.Square),
                 reads=b_qpre, writes=b_sqq)
            for ncx in range(4):
                rope(qpre, b_qpre, qfin[pr], b_qfin[pr], ncx, 2 + ncx % 2)
                yield
            for ncx in range(4):
                P.op("te", lambda o, n=ncx, b=ncx % 4: o.matmul(
                    pb[b][0:1, :], ones_col[:, :], sqq[:, n * 512:(n + 1) * 512],
                    start=True, stop=True),
                    reads=b_sqq + [b_consts], writes=[b_pb[ncx % 4]], tag=f"qn{fc}")
                P.op("ve", lambda o, n=ncx, b=ncx % 4: o.reduce_max(
                    qnt[0:1, n:n + 1], pb[b][0:1, :], axis=AX.X),
                    reads=[b_pb[ncx % 4]], writes=[b_qnt], sync=True)
            yield
            for ncx in range(4):
                qkv_mm(wkb, b_wkb, ncx, ncx % 4)
                P.op("ve", lambda o, n=ncx, f=fc, b=ncx % 4: o.tensor_scalar(
                    kpre[:, n * 512:(n + 1) * 512], pb[b][:, :], bqkv[:, 8 + f:9 + f], None, op0=ALU.add),
                    reads=[b_pb[ncx % 4], b_bqkv], writes=[b_kpre[ncx]])
                yield
            P.op("se", lambda o: o.activation(sqk[:, :], kpre[:, :], AF.Square),
                 reads=b_kpre, writes=b_sqk)
            for ncx in range(4):
                rope(kpre, b_kpre, kfin[pr], b_kfin[pr], ncx, 2 + ncx % 2)
                yield
            for ncx in range(4):
                P.op("te", lambda o, n=ncx, b=ncx % 4: o.matmul(
                    pb[b][0:1, :], ones_col[:, :], sqk[:, n * 512:(n + 1) * 512],
                    start=True, stop=True),
                    reads=b_sqk + [b_consts], writes=[b_pb[ncx % 4]], tag=f"qn{fc}")
                P.op("ve", lambda o, n=ncx, b=ncx % 4: o.reduce_max(
                    qnt[0:1, 4 + n:5 + n], pb[b][0:1, :], axis=AX.X),
                    reads=[b_pb[ncx % 4]], writes=[b_qnt], sync=True)
            yield
            P.op("ve", lambda o: o.reduce_max(qnt[0:1, 8:9], qnt[0:1, 0:4], axis=AX.X),
                 reads=[b_qnt], writes=[b_qnt], sync=True)
            P.op("ve", lambda o: o.reduce_max(qnt[0:1, 9:10], qnt[0:1, 4:8], axis=AX.X),
                 reads=[b_qnt], writes=[b_qnt], sync=True)
            P.op("ve", lambda o: o.tensor_tensor(qnt[0:1, 10:11], qnt[0:1, 8:9], qnt[0:1, 9:10], op=ALU.mult),
                 reads=[b_qnt], writes=[b_qnt], sync=True)
            P.op("se", lambda o: o.activation(qnt[0:1, 13:14], qnt[0:1, 10:11], AF.Ln),
                 reads=[b_qnt], writes=[b_qnt])
            P.op("se", lambda o: o.activation(qnt[0:1, 11:12], qnt[0:1, 13:14], AF.Exp, scale=0.5),
                 reads=[b_qnt], writes=[b_qnt])
            P.op("ve", lambda o: o.tensor_scalar(qnt[0:1, 12:13], qnt[0:1, 11:12], -0.125, None, op0=ALU.mult),
                 reads=[b_qnt], writes=[b_qnt], sync=True)
            P.op("ve", lambda o: o.tensor_copy(qnb[0:1, 0:1], qnt[0:1, 12:13]),
                 reads=[b_qnt], writes=[b_qnb], sync=True)
            for ncx in range(4):
                qkv_mm(wvb, b_wvb, ncx, ncx % 4, nb=3)
                P.op("ve", lambda o, n=ncx, f=fc, b=ncx % 4: o.tensor_scalar(
                    vbf[:, n * 512:(n + 1) * 512], pb[b][:, :], bqkv[:, 16 + f:17 + f],
                    modfm[:, 2 * 8 + f:2 * 8 + f + 1], op0=ALU.add, op1=ALU.mult),
                    reads=[b_pb[ncx % 4], b_bqkv, b_modfm], writes=[b_vbf[ncx]], sync=True)
                yield
            P.op("te", lambda o: o.matmul(pb[2][:, 0:1], ones_row[:, :], qnb[0:1, 0:1], start=True, stop=True),
                 reads=[b_qnb, b_consts], writes=[b_pb[2]])
            P.op("ve", lambda o, pr=pr: o.tensor_copy(expb[:, pr:pr + 1], pb[2][:, 0:1]),
                 reads=[b_pb[2]], writes=[b_expb], sync=True)
            for t in range(16):
                vb_ = 2 + (t % 2)
                P.op("te", lambda o, t=t, b=vb_: o.transpose(
                    pb[b][:, 0:128], vbf[:, t * 128:(t + 1) * 128], identF[:, :]),
                    reads=[b_vbf[t // 4], b_consts], writes=[b_pb[vb_]], tag=f"vT{fc}.{t}")
                P.op("ve", lambda o, t=t, pr=pr, b=vb_: o.tensor_copy(Vbuf[pr][:, t * 130:t * 130 + 64], pb[b][:, 0:64]),
                     reads=[b_pb[vb_]], writes=[b_Vbuf[pr][t]])
                P.op("ve", lambda o, t=t, pr=pr, b=vb_: o.tensor_copy(Vbuf[pr][:, t * 130 + 65:t * 130 + 129], pb[b][:, 64:128]),
                     reads=[b_pb[vb_]], writes=[b_Vbuf[pr][t]])
                if t % 4 == 3:
                    yield

        blk_ctr = [0]
        pending = [None]

        def emit_epilogue(st):
            fc, hh, qc, cb = st
            qs = slice(qc * 512, (qc + 1) * 512)
            P.op("se", lambda o, b=cb: o.activation(recrow[64:65, :], pb[b][64:65, :], AF.Ln,
                                                     bias=cvec[64:65, 1:2]),
                 reads=[b_pb[cb], b_cvec], writes=[b_rec])
            P.op("se", lambda o: o.activation(recb[64:65, :], recrow[64:65, :], AF.Exp, scale=-1.0),
                 reads=[b_rec], writes=[b_recb])
            P.op("te", lambda o: o.matmul(
                pb[6][0:64, :], ones_wide[64:65, 0:64], recb[64:65, :],
                start=True, stop=True),
                 reads=[b_recb, b_consts], writes=[b_pb[6]])
            P.op("ve", lambda o: o.tensor_copy(lrep[0:64, :], pb[6][0:64, :]),
                 reads=[b_pb[6]], writes=[b_lrep])
            P.op("ve", lambda o, b=cb: o.tensor_tensor(ctmp[0:64, :], pb[b][0:64, :], lrep[0:64, :], op=ALU.mult),
                 reads=[b_pb[cb], b_lrep], writes=[b_ctmp])
            if hh == 0:
                P.op("ve", lambda o, f=fc, q=qs: o.tensor_tensor(
                    xT[f][0:64, q], xT[f][0:64, q], ctmp[0:64, :], op=ALU.add),
                    reads=[b_ctmp, b_xT[fc]], writes=[b_xT[fc]], tag=f"xacc{fc}.{hh}.{qc}")
            else:
                P.op("dq", lambda o: o.dma_start(ctxBhi[64:128, :], ctmp[0:64, :]),
                     reads=[b_ctmp], writes=[b_ctxBhi], tag=f"cmv{fc}.{qc}")
                P.op("ve", lambda o, f=fc, q=qs: o.tensor_tensor(
                    xT[f][64:128, q], xT[f][64:128, q], ctxBhi[64:128, :], op=ALU.add),
                    reads=[b_ctxBhi, b_xT[fc]], writes=[b_xT[fc]], tag=f"xaccB{fc}.{qc}")

        def core_block(fc, hh, qc):
            pr = fc % 2
            off = hh * 64
            qs = slice(qc * 512, (qc + 1) * 512)
            base = (blk_ctr[0] * 16) % NPT
            for kc in range(16):
                bank = kc % 4
                P.op("te", lambda o, k=kc, of=off, q=qs, b=bank, pr=pr: o.matmul(
                    pb[b][:, :], kfin[pr][of:of + 64, k * 128:(k + 1) * 128], qfin[pr][of:of + 64, q],
                    start=True, stop=True),
                    reads=[b_kfin[pr][kc // 4], b_qfin[pr][qc], b_consts], writes=[b_pb[bank]],
                    tag=f"sc{fc}.{hh}.{qc}.{kc}")
                if kc % 2 == 1:
                    g = (kc % 4) // 2
                    s0 = (base + kc - 1) % NPT
                    P.op("se", lambda o, g=g, pr=pr, s0=s0: o.activation(
                        PT[:, s0 * 512:(s0 + 2) * 512], pwf[g], AF.Exp,
                        bias=expb[:, pr:pr + 1], scale=0.125),
                        reads=[b_pb[2 * g], b_pb[2 * g + 1], b_expb],
                        writes=[b_PT[s0], b_PT[s0 + 1]], tag=f"exp{fc}.{hh}.{qc}.{kc}")
            cb = 4 + (blk_ctr[0] % 2)
            blk_ctr[0] += 1
            vs = 65 * hh
            for kc in range(16):
                slot = (base + kc) % NPT
                P.op("te", lambda o, k=kc, b=cb, vs=vs, pr=pr, s=slot: o.matmul(
                    pb[b][0:65, :], Vbuf[pr][:, k * 130 + vs:k * 130 + vs + 65],
                    PT[:, s * 512:(s + 1) * 512],
                    start=(k == 0), stop=(k == 15)),
                    reads=[b_Vbuf[pr][kc], b_PT[slot]], writes=[b_pb[cb]], tag=f"ctx{fc}.{hh}.{qc}")
            prev = pending[0]
            pending[0] = (fc, hh, qc, cb)
            if prev is not None:
                emit_epilogue(prev)

        for fc in range(8):
            for _ in prep_pair(fc):
                pass
            for hh in range(2):
                for qc in range(4):
                    core_block(fc, hh, qc)
        emit_epilogue(pending[0])
        P.end_scope()

    def ffn():
        P.begin_scope()
        AR.reset()
        h1s = AR.alloc([128, NHF * 2048], BF16)
        b_h1s = [B(f"h1s{m}") for m in range(NHF)]
        wrep = AR.alloc([128, S], F32); b_wrep = B("wrep")
        logt = AR.alloc([8, S], F32); b_logt = B("logt")
        wsb = AR.alloc([128, 128], F32); b_wsb = [B(f"w{t}") for t in range(16)]
        waug = AR.alloc([9, S], BF16); b_waug = B("waug")
        wrow = AR.alloc([1, S], BF16); b_wrow = B("wrow")
        rb2aug = AR.alloc([9, D], BF16); b_rb2aug = B("rb2aug")
        rw1b = [AR.alloc([128, 8 * 128], BF16) for _ in range(3)]
        b_rw1b = [B("rw1b0"), B("rw1b1"), B("rw1b2")]
        rw2b = [AR.alloc([128, 16 * 128], BF16) for _ in range(2)]
        b_rw2b = [B("rw2b0"), B("rw2b1")]
        ctF = AR.alloc([128, 512], F32); b_ctF = B("ctF")

        P.op("dg", lambda o: o.dma_start(rb2aug[0:8, :], rb2[:, :]), writes=[b_rb2aug])
        P.op("dg", lambda o: o.dma_start(rb2aug[8:9, :], sb2.rearrange("(one n) -> one n", one=1)),
             writes=[b_rb2aug])

        for ncx in range(4):
            cs = slice(ncx * 512, (ncx + 1) * 512)
            for kc in range(NKC):
                P.op("te", lambda o, k=kc, c=cs: o.matmul(
                    pb[0][0:8, :], rwrout[:, k * 8:(k + 1) * 8], hT[k][:, c],
                    start=(k == 0), stop=(k == NKC - 1)),
                    reads=[b_rwrout, b_hT[kc]], writes=[b_pb[0]], tag=f"rout{ncx}")
            P.op("ve", lambda o, c=cs: o.tensor_scalar(
                logt[0:8, c], pb[0][0:8, :], rbrow[0:8, 0:1], None, op0=ALU.add),
                reads=[b_pb[0], b_rbrow], writes=[b_logt])
        for t in range(16):
            P.op("te", lambda o, t=t: o.transpose(
                pb[1][:, 0:8], logt[0:8, t * 128:(t + 1) * 128], identF[0:8, 0:8]),
                reads=[b_logt, b_consts], writes=[b_pb[1]], tag=f"logT{t}")
            P.op("ve", lambda o: o.reduce_max(smscr[:, 0:1], pb[1][:, 0:8], axis=AX.X, negate=True),
                 reads=[b_pb[1]], writes=[b_smscr], sync=True)
            P.op("se", lambda o: o.activation(sme[:, :], pb[1][:, 0:8], AF.Exp, bias=smscr[:, 0:1]),
                 reads=[b_pb[1], b_smscr], writes=[b_sme])
            P.op("ve", lambda o: o.reduce_sum(smscr[:, 1:2], sme[:, :], axis=AX.X),
                 reads=[b_sme], writes=[b_smscr], sync=True)
            P.op("ve", lambda o: o.reduce_max(smscr[:, 2:3], sme[:, :], axis=AX.X),
                 reads=[b_sme], writes=[b_smscr], sync=True)
            P.op("ve", lambda o: o.tensor_scalar(smg[:, :], sme[:, :], smscr[:, 2:3], None, op0=ALU.is_ge),
                 reads=[b_sme, b_smscr], writes=[b_smg], sync=True)
            P.op("ve", lambda o: o.scalar_tensor_tensor(smg[:, :], smg[:, :], -2.0, sme[:, :],
                                                        op0=ALU.mult, op1=ALU.add),
                 reads=[b_sme], writes=[b_smg], sync=True)
            P.op("ve", lambda o: o.reduce_max(smscr[:, 3:4], smg[:, :], axis=AX.X),
                 reads=[b_smg], writes=[b_smscr], sync=True)
            P.op("se", lambda o: o.activation(smscr[:, 5:6], smscr[:, 1:2], AF.Ln),
                 reads=[b_smscr], writes=[b_smscr])
            P.op("se", lambda o: o.activation(smscr[:, 4:5], smscr[:, 5:6], AF.Exp, scale=-1.0),
                 reads=[b_smscr], writes=[b_smscr])
            P.op("ve", lambda o: o.tensor_scalar(smg[:, :], sme[:, :], smscr[:, 3:4], None, op0=ALU.is_ge),
                 reads=[b_sme, b_smscr], writes=[b_smg], sync=True)
            P.op("ve", lambda o, t=t: o.scalar_tensor_tensor(
                wsb[:, t * 8:(t + 1) * 8], sme[:, :], smscr[:, 4:5], smg[:, :],
                op0=ALU.mult, op1=ALU.mult),
                reads=[b_sme, b_smscr, b_smg], writes=[b_wsb[t]], tag=f"w{t}", sync=True)
        P.op("dq", lambda o: o.dma_start(waug[8:9, :], d_orow2k[:, :]), writes=[b_waug])
        for t in range(16):
            P.op("te", lambda o, t=t: o.transpose(
                pb[1][0:8, 0:128], wsb[:, t * 8:(t + 1) * 8], identF[:, :]),
                reads=[b_wsb[t], b_consts], writes=[b_pb[1]], tag=f"wT{t}")
            P.op("ve", lambda o, t=t: o.tensor_copy(waug[0:8, t * 128:(t + 1) * 128], pb[1][0:8, 0:128]),
                 reads=[b_pb[1]], writes=[b_waug])
        for dc in range(NKC):
            for tokc in range(4):
                cs = slice(tokc * 512, (tokc + 1) * 512)
                P.op("te", lambda o, d=dc, c=cs: o.matmul(
                    pb[0][:, :], rb2aug[0:9, d * 128:(d + 1) * 128], waug[0:9, c],
                    start=True, stop=True),
                    reads=[b_rb2aug, b_waug], writes=[b_pb[0]], tag=f"bm{dc}.{tokc}")
                P.op("ve", lambda o, d=dc, c=cs: o.scalar_tensor_tensor(
                    xT[d][:, c], pb[0][:, :], modfm[:, 5 * 8 + d:5 * 8 + d + 1], xT[d][:, c],
                    op0=ALU.mult, op1=ALU.add),
                    reads=[b_pb[0], b_modfm], writes=[b_xT[dc]], tag=f"bme{dc}.{tokc}", sync=True)

        def expert(e):
            shared = (e == -1)
            ei = max(e, 0)
            pw1 = (lambda mc: sw1[:, mc * 128:(mc + 1) * 128]) if shared else \
                  (lambda mc: rw1[ei, :, mc * 128:(mc + 1) * 128])
            pw2 = (lambda dc: sw2[:, dc * 128:(dc + 1) * 128]) if shared else \
                  (lambda dc: rw2[ei, :, dc * 128:(dc + 1) * 128])
            if shared:
                biasc = lambda mc: sb1fm[:, mc:mc + 1]
                b_bias = b_sb1fm
            else:
                half = e % 2
                P.op("dq", lambda o: o.dma_start(stage16[:, :], rb1[ei, :].rearrange("(j p) -> j p", p=128)),
                     writes=[b_stage16], tag=f"rb1{e}")
                P.op("te", lambda o: o.transpose(pb[1][:, 0:16], stage16[:, :], identF[0:16, 0:16]),
                     reads=[b_stage16, b_consts], writes=[b_pb[1]])
                P.op("ve", lambda o, h=half: o.tensor_copy(rb1fm[:, h * 16:(h + 1) * 16], pb[1][:, 0:16]),
                     reads=[b_pb[1]], writes=[b_rb1fm[half]])
                biasc = lambda mc, h=half: rb1fm[:, h * 16 + mc:h * 16 + mc + 1]
                b_bias = b_rb1fm[e % 2]
                P.op("dq", lambda o, e=e: o.dma_start(wrow[0:1, :], waug[e:e + 1, :]),
                     reads=[b_waug], writes=[b_wrow], tag=f"wrow{e}")
                for ncx in range(4):
                    cs = slice(ncx * 512, (ncx + 1) * 512)
                    P.op("te", lambda o, c=cs: o.matmul(pb[0][:, :], ones_row[:, :], wrow[0:1, c],
                                                        start=True, stop=True),
                         reads=[b_wrow, b_consts], writes=[b_pb[0]])
                    P.op("ve", lambda o, c=cs: o.tensor_copy(wrep[:, c], pb[0][:, :]),
                         reads=[b_pb[0]], writes=[b_wrep])
            for mc in range(NHF):
                wb = b_rw1b[mc % 3]
                P.op("dg", lambda o, m=mc: o.dma_start(
                    rw1b[m % 3][:, :].rearrange("p (kc m2) -> p kc m2", kc=8),
                    pw1(m).rearrange("(kc p) m2 -> p kc m2", p=128)),
                    writes=[wb], tag=f"rw1.{e}.{mc}")
                for tokc in range(4):
                    bank = 4 + (tokc % 2)
                    cs = slice(tokc * 512, (tokc + 1) * 512)
                    for kc in range(NKC):
                        P.op("te", lambda o, m=mc, k=kc, c=cs, b=bank: o.matmul(
                            pb[b][:, :], rw1b[m % 3][:, k * 128:(k + 1) * 128], hT[k][:, c],
                            start=(k == 0), stop=(k == NKC - 1)),
                            reads=[wb, b_hT[kc]], writes=[b_pb[bank]], tag=f"h1.{e}.{mc}.{tokc}")
                    P.op("se", lambda o, m=mc, c=cs, b=bank: o.activation(
                        h1s[:, m * 2048 + c.start:m * 2048 + c.stop], pb[b][:, :], AF.Gelu,
                        bias=biasc(m)),
                        reads=[b_pb[bank], b_bias], writes=[b_h1s[mc]], tag=f"gel.{e}.{mc}.{tokc}")
            for dc in range(NKC):
                wb = b_rw2b[dc % 2]
                P.op("dg", lambda o, d=dc: o.dma_start(
                    rw2b[d % 2][:, :].rearrange("p (kc m2) -> p kc m2", kc=16),
                    pw2(d).rearrange("(kc p) m2 -> p kc m2", p=128)),
                    writes=[wb], tag=f"rw2.{e}.{dc}")
                for tokc in range(4):
                    bank = 6 + (tokc % 2)
                    cs = slice(tokc * 512, (tokc + 1) * 512)
                    for kc in range(16):
                        P.op("te", lambda o, d=dc, k=kc, c=cs, b=bank: o.matmul(
                            pb[b][:, :], rw2b[d % 2][:, k * 128:(k + 1) * 128],
                            h1s[:, k * 2048 + c.start:k * 2048 + c.stop],
                            start=(k == 0), stop=(k == 15)),
                            reads=[wb, b_h1s[kc]], writes=[b_pb[bank]], tag=f"o2.{e}.{dc}.{tokc}")
                    if shared:
                        P.op("ve", lambda o, d=dc, c=cs, b=bank: o.scalar_tensor_tensor(
                            xT[d][:, c], pb[b][:, :], modfm[:, 5 * 8 + d:5 * 8 + d + 1], xT[d][:, c],
                            op0=ALU.mult, op1=ALU.add),
                            reads=[b_pb[bank], b_modfm], writes=[b_xT[dc]], tag=f"oacc.{e}.{dc}.{tokc}", sync=True)
                    else:
                        P.op("ve", lambda o, d=dc, c=cs, b=bank: o.scalar_tensor_tensor(
                            ctF[:, :], pb[b][:, :], modfm[:, 5 * 8 + d:5 * 8 + d + 1], wrep[:, c],
                            op0=ALU.mult, op1=ALU.mult),
                            reads=[b_pb[bank], b_modfm, b_wrep], writes=[b_ctF], tag=f"osc.{e}.{dc}.{tokc}", sync=True)
                        P.op("ve", lambda o, d=dc, c=cs: o.tensor_tensor(
                            xT[d][:, c], xT[d][:, c], ctF[:, :], op=ALU.add),
                            reads=[b_ctF, b_xT[dc]], writes=[b_xT[dc]], tag=f"oacc.{e}.{dc}.{tokc}")

        expert(-1)
        for e in range(E):
            expert(e)
        P.end_scope()

    def store():
        P.begin_scope()
        AR.reset()
        xo = [AR.alloc([128, D], F32) for _ in range(2)]
        b_xo = [B("xo0"), B("xo1")]
        for t in range(NTT):
            bo = b_xo[t % 2]
            for kc in range(NKC):
                bank = 6 + (kc % 2)
                P.op("te", lambda o, t=t, k=kc, b=bank: o.transpose(
                    pb[b][:, 0:128], xT[k][:, t * 128:(t + 1) * 128], identF[:, :]),
                    reads=[b_xT[kc], b_consts], writes=[b_pb[bank]], tag=f"xoT{t}.{kc}")
                P.op("ve", lambda o, t=t, k=kc, b=bank: o.tensor_copy(
                    xo[t % 2][:, k * 128:(k + 1) * 128], pb[b][:, 0:128]),
                    reads=[b_pb[bank]], writes=[bo], tag=f"xoe{t}.{kc}")
            P.op("dq", lambda o, t=t: o.dma_start(x_out[t * 128:(t + 1) * 128, :], xo[t % 2][:, :]),
                 reads=[bo], writes=[b_out], tag=f"st{t}")
        P.end_scope()
        P.op("se", lambda o: o.nop(), reads=[b_out], tag="fence")

    reg("xT", xT, b_xT)
    reg("hT", hT, b_hT)
    reg("modfm", modfm, b_modfm)

    for _layer in range(DEPTH):
        if stop_after == "load":
            break
        layer_norm(1)
        if stop_after == "ln1":
            break
        attention()
        if stop_after == "attn":
            break
        layer_norm(2)
        if stop_after == "ln2":
            break
        ffn()
    store()

    tap_outs = []
    for name in taps:
        tens, bufs = tap_reg[name]
        if isinstance(tens, list):
            for i, tt in enumerate(tens):
                od = nc.dram_tensor(f"tap_{name}_{i}", list(tt.shape), tt.dtype, kind="ExternalOutput")
                P.op("dq", lambda o, od=od, tt=tt: o.dma_start(od[:, :], tt[:, :]),
                     reads=[bufs[i]] if isinstance(bufs, list) else [bufs], writes=[b_out])
                tap_outs.append(f"tap_{name}_{i}")
        else:
            od = nc.dram_tensor(f"tap_{name}", list(tens.shape), tens.dtype, kind="ExternalOutput")
            P.op("dq", lambda o, od=od, tt=tens: o.dma_start(od[:, :], tt[:, :]),
                 reads=[bufs], writes=[b_out])
            tap_outs.append(f"tap_{name}")
    if taps:
        P.op("se", lambda o: o.nop(), reads=[b_out], tag="fence2")

    P.finalize()
    with nc.Block() as blk:
        @blk.tensor
        def _(o):
            P.emit("te", o, sems)

        @blk.vector
        def _(o):
            P.emit("ve", o, sems)

        @blk.scalar
        def _(o):
            P.emit("se", o, sems)

        @blk.gpsimd
        def _(o):
            P.emit("dg", o, sems)

        @blk.sync
        def _(o):
            P.emit("dq", o, sems)

    return nc, tap_outs


_IN_NAMES = dict(
    x="L0_attn_hidden_states", c="L0_c", wmod="L0_wmod",
    wq="L0_attn_Wq", bq="L0_attn_bq", wk="L0_attn_Wk", bk="L0_attn_bk",
    wv="L0_attn_Wv", bv="L0_attn_bv",
    router_w="L0_ffn_router_w", router_b="L0_ffn_router_b",
    sw1="L0_ffn_sw1", sb1="L0_ffn_sb1", sw2="L0_ffn_sw2", sb2="L0_ffn_sb2",
    rw1="L0_ffn_rw1", rb1="L0_ffn_rb1", rw2="L0_ffn_rw2", rb2="L0_ffn_rb2",
)


def make_in_maps(inputs):
    full = {k: np.ascontiguousarray(np.asarray(inputs[v], dtype=np.float32))
            for k, v in _IN_NAMES.items()}
    in_maps = []
    for b in range(8):
        m = dict(full)
        m["x"] = np.ascontiguousarray(full["x"][b])
        m["c"] = np.ascontiguousarray(full["c"][b])
        in_maps.append(m)
    return in_maps


def kernel(**inputs):
    nc, _ = build_program()
    in_maps = make_in_maps(inputs)
    res = run_bass_kernel_spmd(nc, in_maps, core_ids=list(range(8)))
    out = np.stack([np.asarray(r["out"], dtype=np.float32) for r in res.results], axis=0)
    return out


# version 18
# speedup vs baseline: 1.1198x; 1.0087x over previous
import numpy as np
import ml_dtypes
import concourse.bass as bass
import concourse.mybir as mybir
from concourse.bass_utils import run_bass_kernel_spmd

F32 = mybir.dt.float32
BF16 = mybir.dt.bfloat16
AF = mybir.ActivationFunctionType
ALU = mybir.AluOpType
AX = mybir.AxisListType

D = 1024
S = 2048
H = 16
HD = 64
E = 8
HF = 2048
NKC = 8
NTT = 16
NQC = 4
NHF = 16
EPS = 1e-5
DEPTH = 1

ENGS = ("te", "ve", "se", "dg", "dq")
STEP = {"te": 1, "ve": 1, "se": 1, "dg": 16, "dq": 16}


class Buf:
    __slots__ = ("w", "r", "name")

    def __init__(self, name=""):
        self.w = {}
        self.r = {}
        self.name = name


class Planner:
    def __init__(self):
        self.ops = {e: [] for e in ENGS}
        self.ghost = {}
        self.scope_bufs = None

    QUEUES = ("dg", "dq")

    def buf(self, name=""):
        b = Buf(name)
        if self.ghost:
            b.r = dict(self.ghost)
        if self.scope_bufs is not None:
            self.scope_bufs.append(b)
        return b

    def begin_scope(self):
        assert self.scope_bufs is None
        self.scope_bufs = []

    def end_scope(self):
        mx = dict(self.ghost)
        for b in self.scope_bufs:
            for e, i in list(b.r.items()) + list(b.w.items()):
                if i > mx.get(e, -1):
                    mx[e] = i
        self.ghost = mx
        self.scope_bufs = None

    def op(self, eng, emit, reads=(), writes=(), tag="", sync=False):
        idx = len(self.ops[eng])
        deps = {}
        inq = eng in self.QUEUES

        def add(e, i, force=False):
            if (e != eng or force or e in self.QUEUES) and i > deps.get(e, -1):
                deps[e] = i

        for b in reads:
            for e, i in b.w.items():
                add(e, i, force=True)
        for b in writes:
            for e, i in b.w.items():
                add(e, i, force=sync)
            for e, i in b.r.items():
                add(e, i, force=True)
        self.ops[eng].append((emit, deps, tag))
        for b in reads:
            if b.r.get(eng, -1) < idx:
                b.r[eng] = idx
        for b in writes:
            b.w[eng] = idx
            b.r = {}
            if inq or len(b.w) > 1:
                pass
            else:
                b.w = {eng: idx}
        return (eng, idx)

    def check_deadlock(self):
        ptr = {e: 0 for e in ENGS}
        total = sum(len(self.ops[e]) for e in ENGS)
        done = 0
        progress = True
        while progress:
            progress = False
            for e in ENGS:
                while ptr[e] < len(self.ops[e]):
                    _, deps, tag = self.ops[e][ptr[e]]
                    if all(ptr[de] > di for de, di in deps.items()):
                        ptr[e] += 1
                        done += 1
                        progress = True
                    else:
                        break
        if done != total:
            stuck = {e: (ptr[e], len(self.ops[e]),
                         self.ops[e][ptr[e]][2] if ptr[e] < len(self.ops[e]) else "")
                     for e in ENGS}
            raise RuntimeError(f"planner deadlock: {stuck}")

    NSEM = 6

    def finalize(self):
        self.check_deadlock()
        incset = {e: set() for e in ENGS}
        for e in ENGS:
            for (_, deps, _) in self.ops[e]:
                for de, di in deps.items():
                    incset[de].add(di)
        self.rank = {}
        self.incset = {}
        for e in ("te", "ve", "se"):
            srt = sorted(incset[e])
            self.incset[e] = set(srt)
            r = {}
            for pos, i in enumerate(srt):
                r[i] = pos + 1
            self.rank[e] = r
        self.qsem = {}
        for e in ("dg", "dq"):
            n = len(self.ops[e])
            self.qsem[e] = [(i % self.NSEM, i // self.NSEM + 1) for i in range(n)]

    def dep_sem(self, e, i):
        if e in ("dg", "dq"):
            slot, k = self.qsem[e][i]
            return (f"{e}{slot}", 16 * k)
        return (e, self.rank[e][i])

    def emit(self, eng, obj, sems):
        waited = {}

        def wait(key, v):
            if v > waited.get(key, 0):
                obj.wait_ge(sems[key], v)
                waited[key] = v

        if eng in ("dg", "dq"):
            for idx, (emit_fn, deps, tag) in enumerate(self.ops[eng]):
                slot, k = self.qsem[eng][idx]
                if k > 1:
                    wait(f"{eng}{slot}", 16 * (k - 1))
                for de, di in deps.items():
                    wait(*self.dep_sem(de, di))
                inst = emit_fn(obj)
                inst.then_inc(sems[f"{eng}{slot}"], 16)
        else:
            inc = self.incset[eng]
            for idx, (emit_fn, deps, tag) in enumerate(self.ops[eng]):
                for de, di in deps.items():
                    wait(*self.dep_sem(de, di))
                inst = emit_fn(obj)
                if idx in inc:
                    inst.then_inc(sems[eng], 1)


class Arena:

    def __init__(self, nc, nbytes):
        r = nc.bump_sbuf(nbytes)
        assert r is not None, "arena alloc failed"
        self.nc = nc
        self.base = r[0]
        self.size = nbytes
        self.cur = 0
        self.n = 0

    def reset(self):
        self.cur = 0

    def alloc(self, shape, dtype):
        nb = int(np.prod(shape[1:])) * mybir.dt.size(dtype)
        nb = (nb + 31) & ~31
        assert self.cur + nb <= self.size, f"arena overflow {self.cur + nb} > {self.size}"
        self.n += 1
        self.last_off = self.base + self.cur
        t = self.nc.alloc_sbuf_tensor_at(f"ar{self.n}", list(shape), dtype,
                                         offset=self.base + self.cur)
        self.cur += nb
        return t


def _host_consts():
    pos = np.arange(S, dtype=np.float64)
    inv = np.power(10000.0, -2.0 * np.arange(HD // 2, dtype=np.float64) / HD)
    ang = pos[:, None] * inv[None, :]
    cos2 = np.repeat(np.cos(ang), 2, axis=-1)
    sin2 = np.repeat(np.sin(ang), 2, axis=-1)
    cosT = np.empty((128, S), dtype=np.float32)
    sinT = np.empty((128, S), dtype=np.float32)
    cosT[0:64, :] = cos2.T
    cosT[64:128, :] = cos2.T
    sinT[0:64, :] = sin2.T
    sinT[64:128, :] = sin2.T
    prot = np.zeros((128, 128), dtype=np.float32)
    for blk in (0, 64):
        for i in range(32):
            prot[blk + 2 * i + 1, blk + 2 * i] = -1.0
            prot[blk + 2 * i, blk + 2 * i + 1] = 1.0
    return (cosT.astype(ml_dtypes.bfloat16), sinT.astype(ml_dtypes.bfloat16),
            prot.astype(ml_dtypes.bfloat16))


def build_program(taps=(), stop_after=None):
    nc = bass.Bass("TRN2", target_bir_lowering=False)
    P = Planner()

    dt = {}
    def din(name, shape):
        dt[name] = nc.dram_tensor(name, list(shape), F32, kind="ExternalInput")
        return dt[name]

    x_in = din("x", (S, D))
    c_in = din("c", (D,))
    wmod = din("wmod", (D, 6 * D))
    wq = din("wq", (D, D)); bq = din("bq", (D,))
    wk = din("wk", (D, D)); bk = din("bk", (D,))
    wv = din("wv", (D, D)); bv = din("bv", (D,))
    router_w = din("router_w", (D, E)); router_b = din("router_b", (E,))
    sw1 = din("sw1", (D, HF)); sb1 = din("sb1", (HF,))
    sw2 = din("sw2", (HF, D)); sb2 = din("sb2", (D,))
    rw1 = din("rw1", (E, D, HF)); rb1 = din("rb1", (E, HF))
    rw2 = din("rw2", (E, HF, D)); rb2 = din("rb2", (E, D))
    x_out = nc.dram_tensor("out", [S, D], F32, kind="ExternalOutput")

    cosT_np, sinT_np, prot_np = _host_consts()
    d_cos = nc.inline_tensor(cosT_np, "c_cos")
    d_sin = nc.inline_tensor(sinT_np, "c_sin")
    d_prot = nc.inline_tensor(prot_np, "c_prot")
    d_idf = nc.inline_tensor(np.eye(128, dtype=np.float32), "c_idf")
    d_idb = nc.inline_tensor(np.eye(128, dtype=ml_dtypes.bfloat16), "c_idb")
    d_or = nc.inline_tensor(np.ones((1, 128), dtype=ml_dtypes.bfloat16), "c_or")
    d_oc = nc.inline_tensor(np.ones((128, 1), dtype=ml_dtypes.bfloat16), "c_oc")
    d_orow2k = nc.inline_tensor(np.ones((1, S), dtype=ml_dtypes.bfloat16), "c_or2k")

    A = nc.alloc_sbuf_tensor
    xT = [A(f"xT{k}", [128, S], F32) for k in range(NKC)]
    hT = [A(f"hT{k}", [128, S], BF16) for k in range(NKC)]
    identF = A("identF", [128, 128], F32)
    identB = A("identB", [128, 128], BF16)
    prot = A("prot", [128, 128], BF16)
    ones_row = A("ones_row", [1, 128], BF16)
    ones_col = A("ones_col", [128, 1], BF16)
    modfm = A("modfm", [128, 6 * 8], F32)
    bqkv = A("bqkv", [128, 24], F32)
    sb1fm = A("sb1fm", [128, 16], F32)
    rb1fm = A("rb1fm", [128, 32], F32)
    rbrow = A("rbrow", [8, 1], F32)
    rwrout = A("rwrout", [128, 64], BF16)
    stage16 = A("stage16", [16, 128], F32)
    fmtmp = A("fmtmp", [128, 16], F32)
    smscr = [A(f"smscr{i}", [128, 8], F32) for i in range(4)]
    sme = [A(f"sme{i}", [128, 8], F32) for i in range(4)]
    smg = [A(f"smg{i}", [128, 8], F32) for i in range(4)]
    qnt = A("qnt", [1, 16], F32)
    qnb = A("qnb", [1, 2], BF16)
    expb = A("expb", [128, 2], F32)
    cvec = A("cvec", [128, 2], F32)
    ones_wide = A("ones_wide", [128, 64], BF16)

    pw = [nc.alloc_psum_tensor(f"pw{i}", [128, 1024], F32) for i in range(2)]
    pb = []
    for i in range(2):
        pb += [pw[i][:, 0:512], pw[i][:, 512:1024]]
    pb += [nc.alloc_psum_tensor(f"pb{i}", [128, 512], F32)[:, :] for i in range(4, 8)]
    pwf = [pw[0][:, :], pw[1][:, :]]

    AR = Arena(nc, 107 * 1024)

    sems = {e: nc.alloc_semaphore(f"sem_{e}") for e in ("te", "ve", "se")}
    for e in ("dg", "dq"):
        for s in range(Planner.NSEM):
            sems[f"{e}{s}"] = nc.alloc_semaphore(f"sem_{e}{s}")

    B = P.buf
    b_xT = [B(f"xT{k}") for k in range(NKC)]
    b_hT = [B(f"hT{k}") for k in range(NKC)]
    b_consts = B("consts")
    b_modfm = B("modfm")
    b_bqkv = B("bqkv")
    b_sb1fm = B("sb1fm")
    b_rb1fm = [B("rb1fm0"), B("rb1fm1")]
    b_rbrow = B("rbrow")
    b_rwrout = B("rwrout")
    b_stage16 = B("stage16")
    b_fmtmp = B("fmtmp")
    b_smscr = [B(f"smscr{i}") for i in range(4)]
    b_sme = [B(f"sme{i}") for i in range(4)]
    b_smg = [B(f"smg{i}") for i in range(4)]
    b_qnt = B("qnt"); b_qnb = B("qnb"); b_expb = B("expb")
    b_cvec = B("cvec")
    b_pb = [B(f"pb{i}") for i in range(8)]
    b_out = B("outdma")

    tap_reg = {}

    def reg(name, tensor, buf):
        tap_reg[name] = (tensor, buf)

    for dst, src in ((identF, d_idf), (identB, d_idb), (prot, d_prot),
                     (ones_row, d_or), (ones_col, d_oc)):
        P.op("dq", lambda o, d=dst, s=src: o.dma_start(d[:, :], s[:, :]),
             writes=[b_consts], tag="const")
    P.op("dq", lambda o: o.dma_start(rbrow[:, :], router_b.rearrange("(p one) -> p one", one=1)),
         writes=[b_rbrow])
    P.op("ve", lambda o: o.memset(cvec[:, 0:1], EPS), writes=[b_cvec])
    P.op("ve", lambda o: o.memset(cvec[:, 1:2], 1e-30), writes=[b_cvec])
    P.op("ve", lambda o: o.memset(ones_wide[:, :], 1.0), writes=[b_consts])
    P.op("dg", lambda o: o.dma_start(
        rwrout[:, :].rearrange("p (kc e) -> p kc e", kc=8),
        router_w.rearrange("(kc p) e -> p kc e", p=128)), writes=[b_rwrout])

    def load_fm(vec_dram, n, dst_ap_fn, bufw, tag):
        P.op("dq", lambda o: o.dma_start(stage16[0:n, :],
             vec_dram.rearrange("(j p) -> j p", p=128)),
             reads=[], writes=[b_stage16], tag=tag)
        P.op("te", lambda o: o.transpose(pb[7][:, 0:n], stage16[0:n, :], identF[0:n, 0:n]),
             reads=[b_stage16, b_consts], writes=[b_pb[7]], tag=tag)
        P.op("ve", lambda o: o.tensor_copy(dst_ap_fn(), pb[7][:, 0:n]),
             reads=[b_pb[7]], writes=[bufw], tag=tag)

    load_fm(bq, 8, lambda: bqkv[:, 0:8], b_bqkv, "bq")
    load_fm(bk, 8, lambda: bqkv[:, 8:16], b_bqkv, "bk")
    load_fm(bv, 8, lambda: bqkv[:, 16:24], b_bqkv, "bv")
    load_fm(sb1, 16, lambda: sb1fm[:, :], b_sb1fm, "sb1")

    P.begin_scope()
    AR.reset()
    xin = [AR.alloc([128, D], F32) for _ in range(3)]
    b_xin = [B(f"xin{i}") for i in range(3)]
    silu_fm = AR.alloc([128, 8], F32); b_silufm = B("silufm")
    silu_rep = AR.alloc([128, 8 * 128], F32); b_silurep = B("silurep")
    wstage = [AR.alloc([128, 8 * 512], F32) for _ in range(2)]
    b_wstage = [B("ws0"), B("ws1")]
    modsb = [AR.alloc([128, 512], F32) for _ in range(2)]
    b_modsb = [B("modsb0"), B("modsb1")]

    P.op("dq", lambda o: o.dma_start(stage16[0:8, :], c_in.rearrange("(j p) -> j p", p=128)),
         writes=[b_stage16])
    P.op("te", lambda o: o.transpose(pb[7][:, 0:8], stage16[0:8, :], identF[0:8, 0:8]),
         reads=[b_stage16, b_consts], writes=[b_pb[7]])
    P.op("se", lambda o: o.activation(silu_fm[:, :], pb[7][:, 0:8], AF.Silu),
         reads=[b_pb[7]], writes=[b_silufm])
    for kc in range(NKC):
        P.op("ve", lambda o, k=kc: o.tensor_scalar(
            silu_rep[:, k * 128:(k + 1) * 128], identF[:, :],
            0.0, silu_fm[:, k:k + 1], op0=ALU.mult, op1=ALU.add),
            reads=[b_silufm, b_consts], writes=[b_silurep])

    def emit_x_tile(t):
        bi = b_xin[t % 3]
        P.op("dq", lambda o, t=t: o.dma_start(xin[t % 3][:, :], x_in[t * 128:(t + 1) * 128, :]),
             writes=[bi], tag=f"xin{t}")
        for kc in range(NKC):
            pbank = b_pb[6]
            P.op("te", lambda o, t=t, k=kc: o.transpose(
                pb[6][:, 0:128], xin[t % 3][:, k * 128:(k + 1) * 128], identF[:, :]),
                reads=[bi, b_consts], writes=[pbank], tag=f"xT{t}.{kc}")
            P.op("se", lambda o, t=t, k=kc: o.copy(
                xT[k][:, t * 128:(t + 1) * 128], pb[6][:, 0:128]),
                reads=[pbank], writes=[b_xT[kc]], tag=f"xTe{t}.{kc}")

    def emit_mod_group(ncg):
        ws = b_wstage[ncg % 2]
        P.op("dg", lambda o, g=ncg: o.dma_start(
            wstage[g % 2][:, :].rearrange("p (kc n) -> p kc n", kc=8),
            wmod[:, g * 512:(g + 1) * 512].rearrange("(kc p) n -> p kc n", p=128)),
            writes=[ws], tag=f"wmod{ncg}")
        for kc in range(NKC):
            P.op("te", lambda o, g=ncg, k=kc: o.matmul(
                pb[5][:, :], silu_rep[:, k * 128:(k + 1) * 128],
                wstage[g % 2][:, k * 512:(k + 1) * 512],
                start=(k == 0), stop=(k == NKC - 1)),
                reads=[b_silurep, ws], writes=[b_pb[5]], tag=f"mod{ncg}")
        P.op("ve", lambda o, g=ncg: o.tensor_copy(modsb[g % 2][:, :], pb[5][:, :]),
             reads=[b_pb[5]], writes=[b_modsb[ncg % 2]], tag=f"modsb{ncg}")
        for j in range(4):
            f = ncg * 4 + j
            v = f // 8
            cc = f % 8
            P.op("te", lambda o, g=ncg, j=j: o.transpose(
                pb[4][:, 0:128], modsb[g % 2][:, j * 128:(j + 1) * 128], identF[:, :]),
                reads=[b_modsb[ncg % 2], b_consts], writes=[b_pb[4]], tag=f"modT{f}")
            addc = 1.0 if v in (1, 4) else 0.0
            P.op("ve", lambda o, v=v, cc=cc, a=addc: o.tensor_scalar(
                modfm[:, v * 8 + cc:v * 8 + cc + 1], pb[4][:, 0:1], a, None, op0=ALU.add),
                reads=[b_pb[4]], writes=[b_modfm], tag=f"modfm{f}")

    for i in range(12):
        emit_mod_group(i)
        emit_x_tile(i)
    for t in range(12, 16):
        emit_x_tile(t)

    for v in (2, 5):
        sl = modfm[:, v * 8:(v + 1) * 8]
        P.op("se", lambda o, sl=sl: o.activation(fmtmp[:, 0:8], sl, AF.Abs),
             reads=[b_modfm], writes=[b_fmtmp])
        P.op("ve", lambda o: o.tensor_scalar(fmtmp[:, 0:8], fmtmp[:, 0:8], 0.25, None, op0=ALU.max),
             writes=[b_fmtmp])
        P.op("ve", lambda o, sl=sl: o.tensor_scalar(
            fmtmp[:, 8:16], sl, 0.0, 2.0, op0=ALU.is_ge, op1=ALU.mult),
            reads=[b_modfm], writes=[b_fmtmp], sync=True)
        P.op("ve", lambda o, sl=sl: o.tensor_tensor(sl, fmtmp[:, 8:16], fmtmp[:, 0:8], op=ALU.mult),
             reads=[b_fmtmp], writes=[b_modfm], sync=True)
        P.op("ve", lambda o, sl=sl: o.tensor_tensor(sl, sl, fmtmp[:, 0:8], op=ALU.subtract),
             reads=[b_fmtmp], writes=[b_modfm], sync=True)
    P.end_scope()

    def layer_norm(which):
        P.begin_scope()
        AR.reset()
        xbf = [AR.alloc([128, 512], BF16) for _ in range(2)]
        sqbf = [AR.alloc([128, 512], BF16) for _ in range(2)]
        b_xbf = [B("xbf0"), B("xbf1")]
        b_sqbf = [B("sqbf0"), B("sqbf1")]
        m2row = AR.alloc([1, 512], F32); b_m2 = B("m2")
        srow = AR.alloc([1, 512], F32); b_sr = B("sr")
        verow = AR.alloc([1, 512], F32); b_ve_ = B("verow")
        r0row = AR.alloc([1, 512], F32); b_r0 = B("r0")
        t1row = AR.alloc([1, 512], F32); b_t1 = B("t1")
        rrow = AR.alloc([1, 512], F32); b_rr = B("rr")
        mrrow = AR.alloc([1, 512], F32); b_mr = B("mr")
        rbfr = AR.alloc([1, 512], BF16); b_rbf = B("rbf")
        mrbfr = AR.alloc([1, 512], BF16); b_mrbf = B("mrbf")
        rrep = AR.alloc([128, S], F32); b_rrep = B("rrep")
        mrrep = AR.alloc([128, S], F32); b_mrrep = B("mrrep")
        ttile = AR.alloc([128, S], F32); b_tt = B("ttile")

        pS, pQ, pR = b_pb[0], b_pb[1], b_pb[2]
        if which == 1:
            ca = lambda k: modfm[:, 1 * 8 + k:1 * 8 + k + 1]
            sa = lambda k: modfm[:, 0 * 8 + k:0 * 8 + k + 1]
        else:
            ca = lambda k: modfm[:, 4 * 8 + k:4 * 8 + k + 1]
            sa = lambda k: modfm[:, 3 * 8 + k:3 * 8 + k + 1]

        for ch in range(NQC):
            cs = slice(ch * 512, (ch + 1) * 512)
            for kc in range(NKC):
                P.op("se", lambda o, k=kc, c=cs: o.copy(xbf[k % 2][:, :], xT[k][:, c]),
                     reads=[b_xT[kc]], writes=[b_xbf[kc % 2]])
                P.op("se", lambda o, k=kc, c=cs: o.activation(sqbf[k % 2][:, :], xT[k][:, c], AF.Square),
                     reads=[b_xT[kc]], writes=[b_sqbf[kc % 2]])
                P.op("te", lambda o, k=kc: o.matmul(pb[0][0:1, :], ones_col[:, :], xbf[k % 2][:, :],
                     start=(k == 0), stop=(k == NKC - 1)),
                     reads=[b_xbf[kc % 2], b_consts], writes=[pS])
                P.op("te", lambda o, k=kc: o.matmul(pb[1][0:1, :], ones_col[:, :], sqbf[k % 2][:, :],
                     start=(k == 0), stop=(k == NKC - 1)),
                     reads=[b_sqbf[kc % 2], b_consts], writes=[pQ])
            P.op("ve", lambda o: o.tensor_scalar(pb[0][0:1, :], pb[0][0:1, :], 1.0 / D, None, op0=ALU.mult),
                 reads=[pS], writes=[pS])
            P.op("ve", lambda o: o.tensor_scalar(pb[1][0:1, :], pb[1][0:1, :], 1.0 / D, None, op0=ALU.mult),
                 reads=[pQ], writes=[pQ])
            P.op("se", lambda o: o.activation(m2row[0:1, :], pb[0][0:1, :], AF.Square),
                 reads=[pS], writes=[b_m2])
            P.op("ve", lambda o: o.tensor_tensor(pb[1][0:1, :], pb[1][0:1, :], m2row[0:1, :], op=ALU.subtract),
                 reads=[pQ, b_m2], writes=[pQ])
            P.op("se", lambda o: o.activation(srow[0:1, :], pb[1][0:1, :], AF.Ln, bias=cvec[0:1, 0:1]),
                 reads=[pQ, b_cvec], writes=[b_sr])
            P.op("se", lambda o: o.activation(rrow[0:1, :], srow[0:1, :], AF.Exp, scale=-0.5),
                 reads=[b_sr], writes=[b_rr])
            P.op("se", lambda o: o.activation(rbfr[0:1, :], srow[0:1, :], AF.Exp, scale=-0.5),
                 reads=[b_sr], writes=[b_rbf])
            P.op("ve", lambda o: o.tensor_tensor(mrrow[0:1, :], pb[0][0:1, :], rrow[0:1, :], op=ALU.mult),
                 reads=[pS, b_rr], writes=[b_mr])
            P.op("ve", lambda o: o.tensor_copy(mrbfr[0:1, :], mrrow[0:1, :]),
                 reads=[b_mr], writes=[b_mrbf])
            P.op("te", lambda o: o.matmul(pb[2][:, :], ones_row[:, :], rbfr[0:1, :], start=True, stop=True),
                 reads=[b_rbf, b_consts], writes=[pR])
            P.op("ve", lambda o, c=cs: o.tensor_copy(rrep[:, c], pb[2][:, :]),
                 reads=[pR], writes=[b_rrep])
            P.op("te", lambda o: o.matmul(pb[2][:, :], ones_row[:, :], mrbfr[0:1, :], start=True, stop=True),
                 reads=[b_mrbf, b_consts], writes=[pR])
            P.op("ve", lambda o, c=cs: o.tensor_copy(mrrep[:, c], pb[2][:, :]),
                 reads=[pR], writes=[b_mrrep])

        for kc in range(NKC):
            P.op("ve", lambda o, k=kc: o.scalar_tensor_tensor(
                ttile[:, :], xT[k][:, :], ca(k), rrep[:, :], op0=ALU.mult, op1=ALU.mult),
                reads=[b_xT[kc], b_modfm, b_rrep], writes=[b_tt], sync=True)
            P.op("ve", lambda o, k=kc: o.scalar_tensor_tensor(
                ttile[:, :], mrrep[:, :], ca(k), ttile[:, :], op0=ALU.mult, op1=ALU.subtract),
                reads=[b_mrrep, b_modfm, b_tt], writes=[b_tt])
            P.op("ve", lambda o, k=kc: o.tensor_scalar(
                hT[k][:, :], ttile[:, :], -1.0, sa(k), op0=ALU.mult, op1=ALU.add),
                reads=[b_tt, b_modfm], writes=[b_hT[kc]], sync=True)
        P.end_scope()

    def attention():
        P.begin_scope()
        AR.reset()
        cosT = AR.alloc([128, S], BF16); b_cos = B("cos")
        sinT = AR.alloc([128, S], BF16); b_sin = B("sin")
        wqb = [AR.alloc([128, 8 * 128], BF16) for _ in range(3)]
        wkb = [AR.alloc([128, 8 * 128], BF16) for _ in range(3)]
        wvb = [AR.alloc([128, 8 * 128], BF16) for _ in range(3)]
        b_wqb = [B(f"wqb{i}") for i in range(3)]
        b_wkb = [B(f"wkb{i}") for i in range(3)]
        b_wvb = [B(f"wvb{i}") for i in range(3)]
        qpre = AR.alloc([128, S], BF16); b_qpre = [B(f"qpre{i}") for i in range(4)]
        kpre = AR.alloc([128, S], BF16); b_kpre = [B(f"kpre{i}") for i in range(4)]
        qfin = [AR.alloc([128, S], BF16) for _ in range(2)]
        kfin = [AR.alloc([128, S], BF16) for _ in range(2)]
        b_qfin = [[B(f"qfin{j}.{i}") for i in range(4)] for j in range(2)]
        b_kfin = [[B(f"kfin{j}.{i}") for i in range(4)] for j in range(2)]
        vbf = AR.alloc([128, S], F32); b_vbf = [B(f"vbf{i}") for i in range(4)]
        rop1 = AR.alloc([128, 512], F32); b_rop1 = B("rop1")
        rop2 = AR.alloc([128, 512], F32); b_rop2 = B("rop2")
        Vbuf = [AR.alloc([128, 16 * 130], BF16) for _ in range(2)]
        b_Vbuf = [[B(f"V{j}.{i}") for i in range(16)] for j in range(2)]
        NPT = 24
        PT = AR.alloc([128, NPT * 512], BF16)
        pt_off = AR.last_off
        b_PT = [B(f"PT{i}") for i in range(NPT)]
        sqq = nc.alloc_sbuf_tensor_at(f"sqq{AR.n}", [128, S], BF16, offset=pt_off)
        sqk = nc.alloc_sbuf_tensor_at(f"sqk{AR.n}", [128, S], BF16, offset=pt_off + 4096)
        b_sqq = [b_PT[0], b_PT[1], b_PT[2], b_PT[3]]
        b_sqk = [b_PT[4], b_PT[5], b_PT[6], b_PT[7]]
        recrow = AR.alloc([128, 512], F32); b_rec = B("rec")
        recb = AR.alloc([128, 512], BF16); b_recb = B("recb")
        lrep = AR.alloc([128, 512], F32); b_lrep = B("lrep")
        ctmp = AR.alloc([128, 512], F32); b_ctmp = B("ctmp")
        ctxBhi = AR.alloc([128, 512], F32); b_ctxBhi = B("ctxBhi")

        if not hasattr(AR, "_rep"):
            AR._rep = True
        P.op("dq", lambda o: o.dma_start(cosT[:, :], d_cos[:, :]), writes=[b_cos])
        P.op("dq", lambda o: o.dma_start(sinT[:, :], d_sin[:, :]), writes=[b_sin])
        for j in range(2):
            for t in range(16):
                P.op("ve", lambda o, j=j, t=t: o.memset(Vbuf[j][:, t * 130 + 64:t * 130 + 65], 1.0),
                     writes=[b_Vbuf[j][t]])
                P.op("ve", lambda o, j=j, t=t: o.memset(Vbuf[j][:, t * 130 + 129:t * 130 + 130], 1.0),
                     writes=[b_Vbuf[j][t]])

        def pair_weights(fc):
            for (w, tb, bb, nb) in ((wq, wqb, b_wqb, 3), (wk, wkb, b_wkb, 3), (wv, wvb, b_wvb, 3)):
                P.op("dg", lambda o, w=w, t=tb[fc % nb], f=fc: o.dma_start(
                    t[:, :].rearrange("p (kc m) -> p kc m", kc=8),
                    w[:, f * 128:(f + 1) * 128].rearrange("(kc p) m -> p kc m", p=128)),
                    writes=[bb[fc % nb]], tag=f"wqkv{fc}")

        def prep_pair(fc):
            pr = fc % 2
            wr = fc % 3
            if fc == 0:
                pair_weights(0)
                pair_weights(1)
            if fc < 6:
                pair_weights(fc + 2)

            def qkv_mm(tb, bb, ncx, bk, nb=3):
                for kc in range(NKC):
                    P.op("te", lambda o, t=tb[fc % nb], k=kc, n=ncx, b=bk: o.matmul(
                        pb[b][:, :], t[:, k * 128:(k + 1) * 128], hT[k][:, n * 512:(n + 1) * 512],
                        start=(k == 0), stop=(k == NKC - 1)),
                        reads=[bb[fc % nb], b_hT[kc]], writes=[b_pb[bk]], tag=f"qkv{fc}.{ncx}")

            def rope(pre, preb, fin, finb, ncx, bk):
                cs = slice(ncx * 512, (ncx + 1) * 512)
                P.op("te", lambda o, p=pre, c=cs, b=bk: o.matmul(
                    pb[b][:, :], prot[:, :], p[:, c], start=True, stop=True),
                    reads=[preb[ncx], b_consts], writes=[b_pb[bk]], tag=f"rot{fc}")
                P.op("ve", lambda o, c=cs, b=bk: o.tensor_tensor(rop1[:, :], pb[b][:, :], sinT[:, c], op=ALU.mult),
                     reads=[b_pb[bk], b_sin], writes=[b_rop1])
                P.op("ve", lambda o, p=pre, c=cs: o.tensor_tensor(rop2[:, :], p[:, c], cosT[:, c], op=ALU.mult),
                     reads=[preb[ncx], b_cos], writes=[b_rop2])
                P.op("ve", lambda o, f=fin, c=cs: o.tensor_tensor(f[:, c], rop1[:, :], rop2[:, :], op=ALU.add),
                     reads=[b_rop1, b_rop2], writes=[finb[ncx]])

            for ncx in range(4):
                qkv_mm(wqb, b_wqb, ncx, ncx % 4)
                P.op("ve", lambda o, n=ncx, f=fc, b=ncx % 4: o.tensor_scalar(
                    qpre[:, n * 512:(n + 1) * 512], pb[b][:, :], bqkv[:, f:f + 1], None, op0=ALU.add),
                    reads=[b_pb[ncx % 4], b_bqkv], writes=[b_qpre[ncx]])
                yield
            for ncx in range(4):
                P.op("se", lambda o, n=ncx: o.activation(
                    sqq[:, n * 512:(n + 1) * 512], qpre[:, n * 512:(n + 1) * 512], AF.Square),
                    reads=[b_qpre[ncx]], writes=[b_sqq[ncx]])
                rope(qpre, b_qpre, qfin[pr], b_qfin[pr], ncx, 2 + ncx % 2)
                yield
            for ncx in range(4):
                P.op("te", lambda o, n=ncx, b=ncx % 4: o.matmul(
                    pb[b][0:1, :], ones_col[:, :], sqq[:, n * 512:(n + 1) * 512],
                    start=True, stop=True),
                    reads=[b_sqq[ncx], b_consts], writes=[b_pb[ncx % 4]], tag=f"qn{fc}")
                P.op("ve", lambda o, n=ncx, b=ncx % 4: o.reduce_max(
                    qnt[0:1, n:n + 1], pb[b][0:1, :], axis=AX.X),
                    reads=[b_pb[ncx % 4]], writes=[b_qnt], sync=True)
            yield
            for ncx in range(4):
                qkv_mm(wkb, b_wkb, ncx, ncx % 4)
                P.op("ve", lambda o, n=ncx, f=fc, b=ncx % 4: o.tensor_scalar(
                    kpre[:, n * 512:(n + 1) * 512], pb[b][:, :], bqkv[:, 8 + f:9 + f], None, op0=ALU.add),
                    reads=[b_pb[ncx % 4], b_bqkv], writes=[b_kpre[ncx]])
                yield
            for ncx in range(4):
                P.op("se", lambda o, n=ncx: o.activation(
                    sqk[:, n * 512:(n + 1) * 512], kpre[:, n * 512:(n + 1) * 512], AF.Square),
                    reads=[b_kpre[ncx]], writes=[b_sqk[ncx]])
                rope(kpre, b_kpre, kfin[pr], b_kfin[pr], ncx, 2 + ncx % 2)
                yield
            for ncx in range(4):
                P.op("te", lambda o, n=ncx, b=ncx % 4: o.matmul(
                    pb[b][0:1, :], ones_col[:, :], sqk[:, n * 512:(n + 1) * 512],
                    start=True, stop=True),
                    reads=[b_sqk[ncx], b_consts], writes=[b_pb[ncx % 4]], tag=f"qn{fc}")
                P.op("ve", lambda o, n=ncx, b=ncx % 4: o.reduce_max(
                    qnt[0:1, 4 + n:5 + n], pb[b][0:1, :], axis=AX.X),
                    reads=[b_pb[ncx % 4]], writes=[b_qnt], sync=True)
            yield
            P.op("ve", lambda o: o.reduce_max(qnt[0:1, 8:9], qnt[0:1, 0:4], axis=AX.X),
                 reads=[b_qnt], writes=[b_qnt], sync=True)
            P.op("ve", lambda o: o.reduce_max(qnt[0:1, 9:10], qnt[0:1, 4:8], axis=AX.X),
                 reads=[b_qnt], writes=[b_qnt], sync=True)
            P.op("ve", lambda o: o.tensor_tensor(qnt[0:1, 10:11], qnt[0:1, 8:9], qnt[0:1, 9:10], op=ALU.mult),
                 reads=[b_qnt], writes=[b_qnt], sync=True)
            P.op("se", lambda o: o.activation(qnt[0:1, 13:14], qnt[0:1, 10:11], AF.Ln),
                 reads=[b_qnt], writes=[b_qnt])
            P.op("se", lambda o: o.activation(qnt[0:1, 11:12], qnt[0:1, 13:14], AF.Exp, scale=0.5),
                 reads=[b_qnt], writes=[b_qnt])
            P.op("ve", lambda o: o.tensor_scalar(qnt[0:1, 12:13], qnt[0:1, 11:12], -0.125, None, op0=ALU.mult),
                 reads=[b_qnt], writes=[b_qnt], sync=True)
            P.op("ve", lambda o: o.tensor_copy(qnb[0:1, 0:1], qnt[0:1, 12:13]),
                 reads=[b_qnt], writes=[b_qnb], sync=True)
            for ncx in range(4):
                qkv_mm(wvb, b_wvb, ncx, ncx % 4, nb=3)
                P.op("ve", lambda o, n=ncx, f=fc, b=ncx % 4: o.tensor_scalar(
                    vbf[:, n * 512:(n + 1) * 512], pb[b][:, :], bqkv[:, 16 + f:17 + f],
                    modfm[:, 2 * 8 + f:2 * 8 + f + 1], op0=ALU.add, op1=ALU.mult),
                    reads=[b_pb[ncx % 4], b_bqkv, b_modfm], writes=[b_vbf[ncx]], sync=True)
                yield
            P.op("te", lambda o: o.matmul(pb[2][:, 0:1], ones_row[:, :], qnb[0:1, 0:1], start=True, stop=True),
                 reads=[b_qnb, b_consts], writes=[b_pb[2]])
            P.op("ve", lambda o, pr=pr: o.tensor_copy(expb[:, pr:pr + 1], pb[2][:, 0:1]),
                 reads=[b_pb[2]], writes=[b_expb], sync=True)
            for t in range(16):
                vb_ = 2 + (t % 2)
                P.op("te", lambda o, t=t, b=vb_: o.transpose(
                    pb[b][:, 0:128], vbf[:, t * 128:(t + 1) * 128], identF[:, :]),
                    reads=[b_vbf[t // 4], b_consts], writes=[b_pb[vb_]], tag=f"vT{fc}.{t}")
                P.op("ve", lambda o, t=t, pr=pr, b=vb_: o.tensor_copy(Vbuf[pr][:, t * 130:t * 130 + 64], pb[b][:, 0:64]),
                     reads=[b_pb[vb_]], writes=[b_Vbuf[pr][t]])
                P.op("ve", lambda o, t=t, pr=pr, b=vb_: o.tensor_copy(Vbuf[pr][:, t * 130 + 65:t * 130 + 129], pb[b][:, 64:128]),
                     reads=[b_pb[vb_]], writes=[b_Vbuf[pr][t]])
                if t % 4 == 3:
                    yield

        blk_ctr = [0]
        pending = [None]

        def emit_epilogue(st):
            fc, hh, qc, cb = st
            qs = slice(qc * 512, (qc + 1) * 512)
            P.op("se", lambda o, b=cb: o.activation(recrow[64:65, :], pb[b][64:65, :], AF.Ln,
                                                     bias=cvec[64:65, 1:2]),
                 reads=[b_pb[cb], b_cvec], writes=[b_rec])
            P.op("se", lambda o: o.activation(recb[64:65, :], recrow[64:65, :], AF.Exp, scale=-1.0),
                 reads=[b_rec], writes=[b_recb])
            P.op("te", lambda o: o.matmul(
                pb[6][0:64, :], ones_wide[64:65, 0:64], recb[64:65, :],
                start=True, stop=True),
                 reads=[b_recb, b_consts], writes=[b_pb[6]])
            P.op("ve", lambda o: o.tensor_copy(lrep[0:64, :], pb[6][0:64, :]),
                 reads=[b_pb[6]], writes=[b_lrep])
            P.op("ve", lambda o, b=cb: o.tensor_tensor(ctmp[0:64, :], pb[b][0:64, :], lrep[0:64, :], op=ALU.mult),
                 reads=[b_pb[cb], b_lrep], writes=[b_ctmp])
            if hh == 0:
                P.op("ve", lambda o, f=fc, q=qs: o.tensor_tensor(
                    xT[f][0:64, q], xT[f][0:64, q], ctmp[0:64, :], op=ALU.add),
                    reads=[b_ctmp, b_xT[fc]], writes=[b_xT[fc]], tag=f"xacc{fc}.{hh}.{qc}")
            else:
                P.op("dq", lambda o: o.dma_start(ctxBhi[64:128, :], ctmp[0:64, :]),
                     reads=[b_ctmp], writes=[b_ctxBhi], tag=f"cmv{fc}.{qc}")
                P.op("ve", lambda o, f=fc, q=qs: o.tensor_tensor(
                    xT[f][64:128, q], xT[f][64:128, q], ctxBhi[64:128, :], op=ALU.add),
                    reads=[b_ctxBhi, b_xT[fc]], writes=[b_xT[fc]], tag=f"xaccB{fc}.{qc}")

        def core_block(fc, hh, qc):
            pr = fc % 2
            off = hh * 64
            qs = slice(qc * 512, (qc + 1) * 512)
            base = (blk_ctr[0] * 16) % NPT
            for kc in range(16):
                bank = kc % 4
                P.op("te", lambda o, k=kc, of=off, q=qs, b=bank, pr=pr: o.matmul(
                    pb[b][:, :], kfin[pr][of:of + 64, k * 128:(k + 1) * 128], qfin[pr][of:of + 64, q],
                    start=True, stop=True),
                    reads=[b_kfin[pr][kc // 4], b_qfin[pr][qc], b_consts], writes=[b_pb[bank]],
                    tag=f"sc{fc}.{hh}.{qc}.{kc}")
                if kc % 2 == 1:
                    g = (kc % 4) // 2
                    s0 = (base + kc - 1) % NPT
                    P.op("se", lambda o, g=g, pr=pr, s0=s0: o.activation(
                        PT[:, s0 * 512:(s0 + 2) * 512], pwf[g], AF.Exp,
                        bias=expb[:, pr:pr + 1], scale=0.125),
                        reads=[b_pb[2 * g], b_pb[2 * g + 1], b_expb],
                        writes=[b_PT[s0], b_PT[s0 + 1]], tag=f"exp{fc}.{hh}.{qc}.{kc}")
            cb = 4 + (blk_ctr[0] % 2)
            blk_ctr[0] += 1
            vs = 65 * hh
            for kc in range(16):
                slot = (base + kc) % NPT
                P.op("te", lambda o, k=kc, b=cb, vs=vs, pr=pr, s=slot: o.matmul(
                    pb[b][0:65, :], Vbuf[pr][:, k * 130 + vs:k * 130 + vs + 65],
                    PT[:, s * 512:(s + 1) * 512],
                    start=(k == 0), stop=(k == 15)),
                    reads=[b_Vbuf[pr][kc], b_PT[slot]], writes=[b_pb[cb]], tag=f"ctx{fc}.{hh}.{qc}")
            prev = pending[0]
            pending[0] = (fc, hh, qc, cb)
            if prev is not None:
                emit_epilogue(prev)

        for fc in range(8):
            for _ in prep_pair(fc):
                pass
            for hh in range(2):
                for qc in range(4):
                    core_block(fc, hh, qc)
        emit_epilogue(pending[0])
        P.end_scope()

    def ffn():
        P.begin_scope()
        AR.reset()
        h1s = AR.alloc([128, NHF * 2048], BF16)
        b_h1s = [B(f"h1s{m}") for m in range(NHF)]
        wrep = AR.alloc([128, S], F32); b_wrep = B("wrep")
        logt = AR.alloc([8, S], F32); b_logt = B("logt")
        wsb = AR.alloc([128, 128], F32); b_wsb = [B(f"w{t}") for t in range(16)]
        waug = AR.alloc([9, S], BF16); b_waug = B("waug")
        wrow = AR.alloc([1, S], BF16); b_wrow = B("wrow")
        rb2aug = AR.alloc([9, D], BF16); b_rb2aug = B("rb2aug")
        rw1b = [AR.alloc([128, 8 * 128], BF16) for _ in range(3)]
        b_rw1b = [B("rw1b0"), B("rw1b1"), B("rw1b2")]
        rw2b = [AR.alloc([128, 16 * 128], BF16) for _ in range(2)]
        b_rw2b = [B("rw2b0"), B("rw2b1")]
        ctF = AR.alloc([128, 512], F32); b_ctF = B("ctF")

        P.op("dg", lambda o: o.dma_start(rb2aug[0:8, :], rb2[:, :]), writes=[b_rb2aug])
        P.op("dg", lambda o: o.dma_start(rb2aug[8:9, :], sb2.rearrange("(one n) -> one n", one=1)),
             writes=[b_rb2aug])

        for ncx in range(4):
            cs = slice(ncx * 512, (ncx + 1) * 512)
            for kc in range(NKC):
                P.op("te", lambda o, k=kc, c=cs: o.matmul(
                    pb[0][0:8, :], rwrout[:, k * 8:(k + 1) * 8], hT[k][:, c],
                    start=(k == 0), stop=(k == NKC - 1)),
                    reads=[b_rwrout, b_hT[kc]], writes=[b_pb[0]], tag=f"rout{ncx}")
            P.op("ve", lambda o, c=cs: o.tensor_scalar(
                logt[0:8, c], pb[0][0:8, :], rbrow[0:8, 0:1], None, op0=ALU.add),
                reads=[b_pb[0], b_rbrow], writes=[b_logt])
        def sm_steps(t):
            j = t % 4
            yield lambda: P.op("te", lambda o, t=t, j=j: o.transpose(
                pb[j][:, 0:8], logt[0:8, t * 128:(t + 1) * 128], identF[0:8, 0:8]),
                reads=[b_logt, b_consts], writes=[b_pb[j]], tag=f"logT{t}")
            yield lambda: P.op("ve", lambda o, j=j: o.reduce_max(smscr[j][:, 0:1], pb[j][:, 0:8], axis=AX.X, negate=True),
                 reads=[b_pb[j]], writes=[b_smscr[j]], sync=True)
            yield lambda: P.op("se", lambda o, j=j: o.activation(sme[j][:, :], pb[j][:, 0:8], AF.Exp, bias=smscr[j][:, 0:1]),
                 reads=[b_pb[j], b_smscr[j]], writes=[b_sme[j]])
            yield lambda: P.op("ve", lambda o, j=j: o.reduce_sum(smscr[j][:, 1:2], sme[j][:, :], axis=AX.X),
                 reads=[b_sme[j]], writes=[b_smscr[j]], sync=True)
            yield lambda: P.op("ve", lambda o, j=j: o.reduce_max(smscr[j][:, 2:3], sme[j][:, :], axis=AX.X),
                 reads=[b_sme[j]], writes=[b_smscr[j]], sync=True)
            yield lambda: P.op("ve", lambda o, j=j: o.tensor_scalar(smg[j][:, :], sme[j][:, :], smscr[j][:, 2:3], None, op0=ALU.is_ge),
                 reads=[b_sme[j], b_smscr[j]], writes=[b_smg[j]], sync=True)
            yield lambda: P.op("ve", lambda o, j=j: o.scalar_tensor_tensor(smg[j][:, :], smg[j][:, :], -2.0, sme[j][:, :],
                                                        op0=ALU.mult, op1=ALU.add),
                 reads=[b_sme[j], b_smg[j]], writes=[b_smg[j]], sync=True)
            yield lambda: P.op("ve", lambda o, j=j: o.reduce_max(smscr[j][:, 3:4], smg[j][:, :], axis=AX.X),
                 reads=[b_smg[j]], writes=[b_smscr[j]], sync=True)
            yield lambda: P.op("se", lambda o, j=j: o.activation(smscr[j][:, 5:6], smscr[j][:, 1:2], AF.Ln),
                 reads=[b_smscr[j]], writes=[b_smscr[j]])
            yield lambda: P.op("se", lambda o, j=j: o.activation(smscr[j][:, 4:5], smscr[j][:, 5:6], AF.Exp, scale=-1.0),
                 reads=[b_smscr[j]], writes=[b_smscr[j]])
            yield lambda: P.op("ve", lambda o, j=j: o.tensor_scalar(smg[j][:, :], sme[j][:, :], smscr[j][:, 3:4], None, op0=ALU.is_ge),
                 reads=[b_sme[j], b_smscr[j]], writes=[b_smg[j]], sync=True)
            yield lambda: P.op("ve", lambda o, t=t, j=j: o.scalar_tensor_tensor(
                wsb[:, t * 8:(t + 1) * 8], sme[j][:, :], smscr[j][:, 4:5], smg[j][:, :],
                op0=ALU.mult, op1=ALU.mult),
                reads=[b_sme[j], b_smscr[j], b_smg[j]], writes=[b_wsb[t]], tag=f"w{t}", sync=True)

        for grp in range(4):
            gens = [sm_steps(grp * 4 + i) for i in range(4)]
            live = list(gens)
            while live:
                nxt = []
                for g in live:
                    try:
                        next(g)()
                        nxt.append(g)
                    except StopIteration:
                        pass
                live = nxt
        P.op("dq", lambda o: o.dma_start(waug[8:9, :], d_orow2k[:, :]), writes=[b_waug])
        for t in range(16):
            P.op("te", lambda o, t=t: o.transpose(
                pb[1][0:8, 0:128], wsb[:, t * 8:(t + 1) * 8], identF[:, :]),
                reads=[b_wsb[t], b_consts], writes=[b_pb[1]], tag=f"wT{t}")
            P.op("ve", lambda o, t=t: o.tensor_copy(waug[0:8, t * 128:(t + 1) * 128], pb[1][0:8, 0:128]),
                 reads=[b_pb[1]], writes=[b_waug])
        for dc in range(NKC):
            for tokc in range(4):
                cs = slice(tokc * 512, (tokc + 1) * 512)
                P.op("te", lambda o, d=dc, c=cs: o.matmul(
                    pb[0][:, :], rb2aug[0:9, d * 128:(d + 1) * 128], waug[0:9, c],
                    start=True, stop=True),
                    reads=[b_rb2aug, b_waug], writes=[b_pb[0]], tag=f"bm{dc}.{tokc}")
                P.op("ve", lambda o, d=dc, c=cs: o.scalar_tensor_tensor(
                    xT[d][:, c], pb[0][:, :], modfm[:, 5 * 8 + d:5 * 8 + d + 1], xT[d][:, c],
                    op0=ALU.mult, op1=ALU.add),
                    reads=[b_pb[0], b_modfm], writes=[b_xT[dc]], tag=f"bme{dc}.{tokc}", sync=True)

        def expert(e):
            shared = (e == -1)
            ei = max(e, 0)
            pw1 = (lambda mc: sw1[:, mc * 128:(mc + 1) * 128]) if shared else \
                  (lambda mc: rw1[ei, :, mc * 128:(mc + 1) * 128])
            pw2 = (lambda dc: sw2[:, dc * 128:(dc + 1) * 128]) if shared else \
                  (lambda dc: rw2[ei, :, dc * 128:(dc + 1) * 128])
            if shared:
                biasc = lambda mc: sb1fm[:, mc:mc + 1]
                b_bias = b_sb1fm
            else:
                half = e % 2
                P.op("dq", lambda o: o.dma_start(stage16[:, :], rb1[ei, :].rearrange("(j p) -> j p", p=128)),
                     writes=[b_stage16], tag=f"rb1{e}")
                P.op("te", lambda o: o.transpose(pb[1][:, 0:16], stage16[:, :], identF[0:16, 0:16]),
                     reads=[b_stage16, b_consts], writes=[b_pb[1]])
                P.op("ve", lambda o, h=half: o.tensor_copy(rb1fm[:, h * 16:(h + 1) * 16], pb[1][:, 0:16]),
                     reads=[b_pb[1]], writes=[b_rb1fm[half]])
                biasc = lambda mc, h=half: rb1fm[:, h * 16 + mc:h * 16 + mc + 1]
                b_bias = b_rb1fm[e % 2]
                P.op("dq", lambda o, e=e: o.dma_start(wrow[0:1, :], waug[e:e + 1, :]),
                     reads=[b_waug], writes=[b_wrow], tag=f"wrow{e}")
                for ncx in range(4):
                    cs = slice(ncx * 512, (ncx + 1) * 512)
                    P.op("te", lambda o, c=cs: o.matmul(pb[0][:, :], ones_row[:, :], wrow[0:1, c],
                                                        start=True, stop=True),
                         reads=[b_wrow, b_consts], writes=[b_pb[0]])
                    P.op("ve", lambda o, c=cs: o.tensor_copy(wrep[:, c], pb[0][:, :]),
                         reads=[b_pb[0]], writes=[b_wrep])
            for mc in range(NHF):
                wb = b_rw1b[mc % 3]
                P.op("dg", lambda o, m=mc: o.dma_start(
                    rw1b[m % 3][:, :].rearrange("p (kc m2) -> p kc m2", kc=8),
                    pw1(m).rearrange("(kc p) m2 -> p kc m2", p=128)),
                    writes=[wb], tag=f"rw1.{e}.{mc}")
                for tokc in range(4):
                    bank = 4 + (tokc % 2)
                    cs = slice(tokc * 512, (tokc + 1) * 512)
                    for kc in range(NKC):
                        P.op("te", lambda o, m=mc, k=kc, c=cs, b=bank: o.matmul(
                            pb[b][:, :], rw1b[m % 3][:, k * 128:(k + 1) * 128], hT[k][:, c],
                            start=(k == 0), stop=(k == NKC - 1)),
                            reads=[wb, b_hT[kc]], writes=[b_pb[bank]], tag=f"h1.{e}.{mc}.{tokc}")
                    P.op("se", lambda o, m=mc, c=cs, b=bank: o.activation(
                        h1s[:, m * 2048 + c.start:m * 2048 + c.stop], pb[b][:, :], AF.Gelu,
                        bias=biasc(m)),
                        reads=[b_pb[bank], b_bias], writes=[b_h1s[mc]], tag=f"gel.{e}.{mc}.{tokc}")
            for dc in range(NKC):
                wb = b_rw2b[dc % 2]
                P.op("dg", lambda o, d=dc: o.dma_start(
                    rw2b[d % 2][:, :].rearrange("p (kc m2) -> p kc m2", kc=16),
                    pw2(d).rearrange("(kc p) m2 -> p kc m2", p=128)),
                    writes=[wb], tag=f"rw2.{e}.{dc}")
                for tokc in range(4):
                    bank = 6 + (tokc % 2)
                    cs = slice(tokc * 512, (tokc + 1) * 512)
                    for kc in range(16):
                        P.op("te", lambda o, d=dc, k=kc, c=cs, b=bank: o.matmul(
                            pb[b][:, :], rw2b[d % 2][:, k * 128:(k + 1) * 128],
                            h1s[:, k * 2048 + c.start:k * 2048 + c.stop],
                            start=(k == 0), stop=(k == 15)),
                            reads=[wb, b_h1s[kc]], writes=[b_pb[bank]], tag=f"o2.{e}.{dc}.{tokc}")
                    if shared:
                        P.op("ve", lambda o, d=dc, c=cs, b=bank: o.scalar_tensor_tensor(
                            xT[d][:, c], pb[b][:, :], modfm[:, 5 * 8 + d:5 * 8 + d + 1], xT[d][:, c],
                            op0=ALU.mult, op1=ALU.add),
                            reads=[b_pb[bank], b_modfm], writes=[b_xT[dc]], tag=f"oacc.{e}.{dc}.{tokc}", sync=True)
                    else:
                        P.op("ve", lambda o, d=dc, c=cs, b=bank: o.scalar_tensor_tensor(
                            ctF[:, :], pb[b][:, :], modfm[:, 5 * 8 + d:5 * 8 + d + 1], wrep[:, c],
                            op0=ALU.mult, op1=ALU.mult),
                            reads=[b_pb[bank], b_modfm, b_wrep], writes=[b_ctF], tag=f"osc.{e}.{dc}.{tokc}", sync=True)
                        P.op("ve", lambda o, d=dc, c=cs: o.tensor_tensor(
                            xT[d][:, c], xT[d][:, c], ctF[:, :], op=ALU.add),
                            reads=[b_ctF, b_xT[dc]], writes=[b_xT[dc]], tag=f"oacc.{e}.{dc}.{tokc}")

        expert(-1)
        for e in range(E):
            expert(e)
        P.end_scope()

    def store():
        P.begin_scope()
        AR.reset()
        xo = [AR.alloc([128, D], F32) for _ in range(2)]
        b_xo = [B("xo0"), B("xo1")]
        for t in range(NTT):
            bo = b_xo[t % 2]
            for kc in range(NKC):
                bank = 6 + (kc % 2)
                P.op("te", lambda o, t=t, k=kc, b=bank: o.transpose(
                    pb[b][:, 0:128], xT[k][:, t * 128:(t + 1) * 128], identF[:, :]),
                    reads=[b_xT[kc], b_consts], writes=[b_pb[bank]], tag=f"xoT{t}.{kc}")
                P.op("ve", lambda o, t=t, k=kc, b=bank: o.tensor_copy(
                    xo[t % 2][:, k * 128:(k + 1) * 128], pb[b][:, 0:128]),
                    reads=[b_pb[bank]], writes=[bo], tag=f"xoe{t}.{kc}")
            P.op("dq", lambda o, t=t: o.dma_start(x_out[t * 128:(t + 1) * 128, :], xo[t % 2][:, :]),
                 reads=[bo], writes=[b_out], tag=f"st{t}")
        P.end_scope()
        P.op("se", lambda o: o.nop(), reads=[b_out], tag="fence")

    reg("xT", xT, b_xT)
    reg("hT", hT, b_hT)
    reg("modfm", modfm, b_modfm)

    for _layer in range(DEPTH):
        if stop_after == "load":
            break
        layer_norm(1)
        if stop_after == "ln1":
            break
        attention()
        if stop_after == "attn":
            break
        layer_norm(2)
        if stop_after == "ln2":
            break
        ffn()
    store()

    tap_outs = []
    for name in taps:
        tens, bufs = tap_reg[name]
        if isinstance(tens, list):
            for i, tt in enumerate(tens):
                od = nc.dram_tensor(f"tap_{name}_{i}", list(tt.shape), tt.dtype, kind="ExternalOutput")
                P.op("dq", lambda o, od=od, tt=tt: o.dma_start(od[:, :], tt[:, :]),
                     reads=[bufs[i]] if isinstance(bufs, list) else [bufs], writes=[b_out])
                tap_outs.append(f"tap_{name}_{i}")
        else:
            od = nc.dram_tensor(f"tap_{name}", list(tens.shape), tens.dtype, kind="ExternalOutput")
            P.op("dq", lambda o, od=od, tt=tens: o.dma_start(od[:, :], tt[:, :]),
                 reads=[bufs], writes=[b_out])
            tap_outs.append(f"tap_{name}")
    if taps:
        P.op("se", lambda o: o.nop(), reads=[b_out], tag="fence2")

    P.finalize()
    with nc.Block() as blk:
        @blk.tensor
        def _(o):
            P.emit("te", o, sems)

        @blk.vector
        def _(o):
            P.emit("ve", o, sems)

        @blk.scalar
        def _(o):
            P.emit("se", o, sems)

        @blk.gpsimd
        def _(o):
            P.emit("dg", o, sems)

        @blk.sync
        def _(o):
            P.emit("dq", o, sems)

    return nc, tap_outs


_IN_NAMES = dict(
    x="L0_attn_hidden_states", c="L0_c", wmod="L0_wmod",
    wq="L0_attn_Wq", bq="L0_attn_bq", wk="L0_attn_Wk", bk="L0_attn_bk",
    wv="L0_attn_Wv", bv="L0_attn_bv",
    router_w="L0_ffn_router_w", router_b="L0_ffn_router_b",
    sw1="L0_ffn_sw1", sb1="L0_ffn_sb1", sw2="L0_ffn_sw2", sb2="L0_ffn_sb2",
    rw1="L0_ffn_rw1", rb1="L0_ffn_rb1", rw2="L0_ffn_rw2", rb2="L0_ffn_rb2",
)


def make_in_maps(inputs):
    full = {k: np.ascontiguousarray(np.asarray(inputs[v], dtype=np.float32))
            for k, v in _IN_NAMES.items()}
    in_maps = []
    for b in range(8):
        m = dict(full)
        m["x"] = np.ascontiguousarray(full["x"][b])
        m["c"] = np.ascontiguousarray(full["c"][b])
        in_maps.append(m)
    return in_maps


def kernel(**inputs):
    nc, _ = build_program()
    in_maps = make_in_maps(inputs)
    res = run_bass_kernel_spmd(nc, in_maps, core_ids=list(range(8)))
    out = np.stack([np.asarray(r["out"], dtype=np.float32) for r in res.results], axis=0)
    return out
